# version 23
# baseline (speedup 1.0000x reference)
import math
from contextlib import ExitStack

import ml_dtypes
import numpy as np

import concourse.bass as bass
import concourse.mybir as mybir
from concourse.ap import AP as _AP
from concourse.bass_utils import run_bass_kernel_spmd

F32 = mybir.dt.float32
BF16 = mybir.dt.bfloat16
I32 = mybir.dt.int32
AF = mybir.ActivationFunctionType
ALU = mybir.AluOpType
AX = mybir.AxisListType

D = 4096
NCORES = 8
T = 2048
NT = 16
NGRP = 4
GW = 512
NIC = 32
HD = 128
NH = 32
NKV = 8
EPS = 1e-5
ISQ = 1.0 / math.sqrt(HD)

KV0_K, KV0_V, KV0_N, KV0_ROWS = 0, 4096, 8192, 8320
KV1_K, KV1_V, KV1_N, KV1_ROWS = 0, 1024, 2048, 2176

BF_COLS = 89088
F32_COLS = 8192
BA = 0
BB = 65536
BC = 82944
FP = 0
FS = 512

P_ONES = 0
P_ONES1P = 1
P_MAXQ2 = 136
P_BQ = 200
P_IDEN = 328

DBG_ATTN = False
DBG_FFN = False
PARANOID = set()



def _masks_for_parity(r):
    masks = np.zeros((8, 128, 512), dtype=np.float32)
    k = np.arange(128)[:, None]
    c = np.arange(512)[None, :]
    rhs = 2 * (c // 128) * 128 + (c % 128)
    for m in range(8):
        d = m - r
        masks[m] = 1.0 if d < 0 else (d * 128 + k <= rhs).astype(np.float32)
    return masks.astype(ml_dtypes.bfloat16)


def _rope_tables(positions):
    inv_freq = 1.0 / (10000.0 ** (np.arange(0, HD, 2, dtype=np.float64) / HD))
    freqs = positions[None, :].astype(np.float64) * inv_freq[:, None]
    cos = np.cos(freqs)
    sin = np.sin(freqs)
    cos_full = np.concatenate([cos, cos], axis=0).astype(np.float32)
    sin_swap = np.concatenate([sin, -sin], axis=0).astype(np.float32)
    return cos_full, sin_swap


def _tile_w(w):
    inn, out = w.shape
    return np.ascontiguousarray(
        w.reshape(inn // 128, 128, out // 128, 128)
        .transpose(2, 1, 0, 3).reshape(out // 128, 128, inn * 128 // 128))


def _prep(inputs):
    f32 = np.float32
    c = np.ascontiguousarray
    sh = {}
    sh["wqT"] = _tile_w(np.asarray(inputs["L0_attn_Wq"], f32).T)
    sh["wkT"] = _tile_w(np.asarray(inputs["L0_attn_Wk"], f32).T)
    sh["wvT"] = _tile_w(np.asarray(inputs["L0_attn_Wv"], f32).T)
    sh["woT"] = _tile_w(np.asarray(inputs["L0_attn_Wo"], f32).T)
    sh["b4"] = c(np.stack([
        np.asarray(inputs["L0_attn_bq"], f32).reshape(NIC, 128).T,
        np.asarray(inputs["L0_attn_bk"], f32).reshape(NIC, 128).T,
        np.asarray(inputs["L0_attn_bv"], f32).reshape(NIC, 128).T,
        np.asarray(inputs["L0_attn_bo"], f32).reshape(NIC, 128).T,
    ], axis=0))
    sh["basewT"] = _tile_w(np.asarray(inputs["L0_ffn_base_W"], f32).T)
    sh["gatewT"] = c(np.asarray(inputs["L0_ffn_gate_W"], f32).T)
    sh["a2T"] = c(np.asarray(inputs["L0_ffn_lora_A"], f32).reshape(224, D).T)
    sh["b2"] = c(np.asarray(inputs["L0_ffn_lora_B"], f32).transpose(0, 2, 1).reshape(224, D))
    sh["qwq"] = _tile_w(np.asarray(inputs["L1_attn_qw_q"], np.int32))
    sh["qwk"] = _tile_w(np.asarray(inputs["L1_attn_qw_k"], np.int32))
    sh["qwv"] = _tile_w(np.asarray(inputs["L1_attn_qw_v"], np.int32))
    sh["qwo"] = _tile_w(np.asarray(inputs["L1_attn_qw_o"], np.int32))

    def _tile_sc(scm):
        g, out = scm.shape
        return np.ascontiguousarray(
            scm.reshape(g, out // 128, 128).transpose(1, 0, 2).reshape(out // 128, g * 128))

    sh["scq"] = _tile_sc(np.asarray(inputs["L1_attn_scales_q"], f32))
    sh["sck"] = _tile_sc(np.asarray(inputs["L1_attn_scales_k"], f32))
    sh["scv"] = _tile_sc(np.asarray(inputs["L1_attn_scales_v"], f32))
    sh["sco"] = _tile_sc(np.asarray(inputs["L1_attn_scales_o"], f32))
    sh["a1T"] = _tile_w(np.asarray(inputs["L1_ffn_A_w"], f32).T)
    sh["b1T"] = c(np.asarray(inputs["L1_ffn_B_w"], f32).T)
    sh["wqwT"] = c(np.asarray(inputs["L1_ffn_Wq_w"], f32).T)
    sh["keysT"] = c(np.asarray(inputs["L1_ffn_keys"], f32).T)

    sh["iden"] = np.eye(128, dtype=f32)
    ex28 = np.zeros((28, 224), f32)
    for e in range(28):
        ex28[e, e * 8:(e + 1) * 8] = 2.0
    sh["ex28"] = ex28
    ex64 = np.zeros((64, 512), f32)
    for e in range(64):
        ex64[e, e * 8:(e + 1) * 8] = 2.0
    sh["ex64"] = ex64

    x = np.asarray(inputs["L0_attn_x"], f32)
    percore = []
    for core in range(NCORES):
        s, r = core // 2, core % 2
        tiles = [r + 2 * i for i in range(NT)]
        tok = np.concatenate([np.arange(g * 128, (g + 1) * 128) for g in tiles])
        d = {"x0": c(x[s][tok].T)}
        cos, sin = _rope_tables(tok)
        d["ropec"] = cos
        d["ropes"] = sin
        d["masks"] = _masks_for_parity(r)
        percore.append(d)
    return sh, percore



class _Sem:
    __slots__ = ("h", "cnt")

    def __init__(self, h):
        self.h = h
        self.cnt = 0


class Rec:
    ENGINES = ("sync", "gpsimd", "tensor", "vector", "scalar")

    def __init__(self, nc, sems):
        self.nc = nc
        self.streams = {e: [] for e in self.ENGINES}
        self.s = sems
        self.qsem = {"sync": sems["qs"], "gpsimd": sems["qg"], "scalar": sems["qa"]}

    def wait(self, eng, *deps):
        for dep in deps:
            if dep is None:
                continue
            q, v = dep
            if v <= 0:
                continue
            self.streams[eng].append(lambda e, h=q.h, v=v: e.wait_ge(h, v))

    _ESEM = {"tensor": "te", "vector": "dve", "scalar": "act"}
    paranoid = False

    SYNC_STREAMS = ("qs", "sx0", "sx1", "ss0", "ss1", "ss2", "ss3",
                    "skv0", "skv1", "sqt0", "sqt1", "sv0", "sv1", "sc0", "sc1")
    GP_STREAMS = ("qg", "sw0", "sw1")

    def _paranoid_wait(self, eng):
        for nm in self.SYNC_STREAMS + self.GP_STREAMS + ("qa", "te", "dve", "act"):
            sem = self.s[nm]
            if sem.cnt > 0:
                self.streams[eng].append(
                    lambda e, h=sem.h, v=sem.cnt: e.wait_ge(h, v))

    def op(self, eng, fn, inc=None, n=1, waits=(), fence=False):
        if self.paranoid:
            self._paranoid_wait(eng)
            if inc is None:
                inc = self.s[self._ESEM[eng]]
            fence = True
        self.wait(eng, *waits)
        if fence and inc is None:
            inc = self.s[self._ESEM[eng]]
        if inc is None:
            self.streams[eng].append(fn)
            return None
        self.streams[eng].append(lambda e, fn=fn, h=inc.h, n=n: fn(e).then_inc(h, n))
        inc.cnt += n
        mark = (inc, inc.cnt)
        if fence:
            self.wait(eng, mark)
        return mark

    def dma(self, eng, out, in_, waits=(), tr=False, sem=None):
        q = sem if sem is not None else self.qsem[eng]
        if self.paranoid:
            self._paranoid_wait(eng)
        self.wait(eng, *waits)
        if tr:
            self.streams[eng].append(
                lambda e, o=out, i=in_, h=q.h: e.dma_start_transpose(o, i).then_inc(h, 16))
        else:
            self.streams[eng].append(
                lambda e, o=out, i=in_, h=q.h: e.dma_start(out=o, in_=i).then_inc(h, 16))
        q.cnt += 16
        return (q, q.cnt)

    def barrier(self):
        marks = {}
        bt = self.btile
        bs = self.bscrap
        pw = self.psw
        for eng, nms in (("sync", self.SYNC_STREAMS), ("gpsimd", self.GP_STREAMS)):
            for nm in nms:
                sem = self.s[nm]
                if sem.cnt > 0:
                    self.streams[eng].append(
                        lambda e, h=sem.h, v=sem.cnt: e.wait_ge(h, v))
            sem = self.s[nms[0]]
            self.streams[eng].append(lambda e, h=sem.h: e.sem_inc(h, 16))
            sem.cnt += 16
            marks[eng] = (sem, sem.cnt)
        dve = self.s["dve"]
        self.streams["vector"].append(
            lambda e, h=dve.h: e.memset(bt, 1.0).then_inc(h, 1))
        dve.cnt += 1
        marks["vector"] = (dve, dve.cnt)
        te = self.s["te"]
        self.wait("tensor", marks["vector"])
        self.streams["tensor"].append(
            lambda e, h=te.h: e.matmul(out=pw, lhsT=bt, rhs=bt,
                                       start=True, stop=True).then_inc(h, 1))
        te.cnt += 1
        marks["tensor"] = (te, te.cnt)
        act = self.s["act"]
        self.wait("scalar", marks["vector"])
        self.streams["scalar"].append(
            lambda e, h=act.h: e.activation(out=bs, in_=bt,
                                            func=AF.Copy).then_inc(h, 1))
        act.cnt += 1
        marks["scalar"] = (act, act.cnt)
        for eng in self.ENGINES:
            self.wait(eng, *[m for k, m in marks.items() if k != eng])


class IO:
    def __init__(self, nc):
        self.nc = nc
        self.tensors = {}
        self.kinds = {}

    def t(self, name, shape=None, dtype=None, kind="Internal"):
        if name in self.tensors:
            return self.tensors[name]
        h = self.nc.dram_tensor(name, list(shape), dtype, kind=kind)
        self.tensors[name] = h
        self.kinds[name] = kind
        return h

    def ap(self, name, shape=None, dtype=None, kind="Internal"):
        return self.t(name, shape, dtype, kind).ap()

    @property
    def input_names(self):
        return [n for n, k in self.kinds.items() if k == "ExternalInput"]

    @property
    def output_names(self):
        return [n for n, k in self.kinds.items() if k == "ExternalOutput"]


class Bufs:
    def __init__(self, bf, f32, ps):
        self.bf = bf
        self.f32 = f32
        self.ps = ps


def _wtile_src(w_ap, oc, width=128):
    return w_ap.rearrange("(ic p) o -> p ic o", p=128)[:, :, oc * width:(oc + 1) * width]



def ph_prologue(rec, bufs, io):
    f32 = bufs.f32
    b4 = io.ap("b4", (4, 128, 32), F32, "ExternalInput")
    iden = io.ap("iden", (128, 128), F32, "ExternalInput")
    rec.dma("sync", f32[:, FP + P_BQ:FP + P_BQ + 128], b4.rearrange("b p c -> p b c"))
    rec.dma("sync", f32[:, FP + P_IDEN:FP + P_IDEN + 128], iden)
    rec.op("vector", lambda e: e.memset(f32[:, FP + P_ONES:FP + P_ONES + 1], 1.0))
    rec.op("vector", lambda e: e.memset(f32[0:1, FP + P_ONES1P:FP + P_ONES1P + 128], 1.0))
    rec.barrier()


def ph_ln(rec, bufs, io, x_name):
    rec.paranoid = "ln" in PARANOID
    s = rec.s
    bf, f32, ps = bufs.bf, bufs.f32, bufs.ps
    x = io.tensors[x_name].ap()
    xt = [f32[:, FS + k * 512:FS + (k + 1) * 512] for k in range(3)]
    sq2 = [bf[:, BC:BC + 512], bf[:, BC + 512:BC + 1024]]
    r_bc = [bf[:, BC + 1024 + k * 512:BC + 1536 + k * 512] for k in range(2)]
    mr_bc = [bf[:, BC + 2048 + k * 512:BC + 2560 + k * 512] for k in range(2)]
    ones_b = bf[:, BC + 3072:BC + 3073]
    m_row = [f32[0:1, FS + 1536 + k * 512:FS + 2048 + k * 512] for k in range(2)]
    v_row = [f32[0:1, FS + 2560 + k * 512:FS + 3072 + k * 512] for k in range(2)]
    m2_row = [f32[0:1, FS + 3584 + k * 512:FS + 4096 + k * 512] for k in range(2)]
    r_row = [f32[0:1, FS + 4608 + k * 512:FS + 5120 + k * 512] for k in range(2)]
    mr_row = [f32[0:1, FS + 5632 + k * 512:FS + 6144 + k * 512] for k in range(2)]
    ones1p = f32[0:1, FP + P_ONES1P:FP + P_ONES1P + 128]

    ones_rdy = rec.op("vector", lambda e: e.memset(ones_b, 1.0), inc=s["dve"])

    _Q = ("sync", "gpsimd", "scalar")
    _QS = ("sx0", "sx1", "sc0")
    ld = [None, None, None]
    cvt_done = [None, None, None]
    sqmm_done = [None, None]
    stats_mm = [None] * NGRP
    grp_cvt = [None] * NGRP
    rows_read = [None] * NGRP
    c_mark = [None, None]
    apply_last = [None, None]
    _PR = lambda g: (g % 3) * 32

    def stats(grp):
        gsl = slice(grp * GW, (grp + 1) * GW)
        mm_sum = mm_sq = None
        for ic in range(NIC):
            b = ic % 3
            b2 = ic % 2
            ld[b] = rec.dma(_Q[b], xt[b], x[ic * 128:(ic + 1) * 128, gsl],
                            waits=(cvt_done[b],), sem=s[_QS[b]])
            xsl = bf[:, BA + ic * T + grp * GW: BA + ic * T + (grp + 1) * GW]
            cvt = rec.op("scalar", lambda e, i=xt[b], o=xsl: e.activation(
                out=o, in_=i, func=AF.Copy), inc=s["act"],
                waits=(ld[b],))
            cvt_done[b] = cvt
            sqa = rec.op("scalar", lambda e, i=xsl, o=sq2[b2]: e.activation(
                out=o, in_=i, func=AF.Square), inc=s["act"],
                waits=(sqmm_done[b2],))
            w0 = (cvt,)
            if ic == 0:
                w0 += (ones_rdy,)
                if grp == 3:
                    w0 += (rows_read[0],)
            mm_sum = rec.op("tensor",
                            lambda e, i=xsl, p=_PR(grp), f=(ic == 0), l=(ic == NIC - 1):
                            e.matmul(out=ps[6][p:p + 1, :], lhsT=ones_b, rhs=i,
                                     start=f, stop=l),
                            inc=s["te"], waits=w0)
            mm_sq = rec.op("tensor",
                           lambda e, o=sq2[b2], p=_PR(grp), f=(ic == 0), l=(ic == NIC - 1):
                           e.matmul(out=ps[7][p:p + 1, :], lhsT=ones_b, rhs=o,
                                    start=f, stop=l),
                           inc=s["te"], waits=(sqa,))
            sqmm_done[b2] = mm_sq
        stats_mm[grp] = (mm_sum, mm_sq)
        grp_cvt[grp] = tuple(cvt_done)

    def rows_bcast(grp):
        pp = grp % 2
        mr, vr, m2, rr, mrr = (m_row[pp], v_row[pp], m2_row[pp],
                               r_row[pp], mr_row[pp])
        rec.op("vector", lambda e, p=_PR(grp): e.tensor_scalar(
            out=mr, in0=ps[6][p:p + 1, :], scalar1=1.0 / D, scalar2=None,
            op0=ALU.mult), waits=stats_mm[grp], fence=True)
        rows_read[grp] = rec.op("vector", lambda e, p=_PR(grp): e.tensor_scalar(
            out=vr, in0=ps[7][p:p + 1, :], scalar1=1.0 / D, scalar2=EPS,
            op0=ALU.mult, op1=ALU.add), inc=s["dve"], fence=True)
        rec.op("vector", lambda e: e.tensor_mul(m2, mr, mr), fence=True)
        dv = rec.op("vector", lambda e: e.tensor_sub(vr, vr, m2),
                    inc=s["dve"], fence=True)
        la = rec.op("scalar", lambda e: e.activation(
            out=rr, in_=vr, func=AF.Ln), inc=s["act"], waits=(dv,), fence=True)
        ea = rec.op("scalar", lambda e: e.activation(
            out=rr, in_=rr, func=AF.Exp, scale=-0.5), inc=s["act"])
        dmr = rec.op("vector", lambda e: e.tensor_mul(mrr, mr, rr),
                     inc=s["dve"], waits=(ea,))
        t1 = rec.op("tensor", lambda e: e.matmul(
            out=ps[4][:, :], lhsT=ones1p, rhs=rr, start=True, stop=True),
            inc=s["te"], waits=(dmr, c_mark[pp]))
        t2 = rec.op("tensor", lambda e: e.matmul(
            out=ps[5][:, :], lhsT=ones1p, rhs=mrr, start=True, stop=True),
            inc=s["te"])
        rec.op("vector", lambda e, o=r_bc[pp]: e.tensor_copy(out=o, in_=ps[4][:, :]),
               inc=s["dve"], waits=(t1, t2, apply_last[pp]))
        c_mark[pp] = rec.op("vector", lambda e, o=mr_bc[pp]: e.tensor_copy(
            out=o, in_=ps[5][:, :]), inc=s["dve"])

    def apply(grp):
        pp = grp % 2
        last = None
        for ic in range(NIC):
            xsl = bf[:, BA + ic * T + grp * GW: BA + ic * T + (grp + 1) * GW]
            rec.op("vector", lambda e, i=xsl: e.tensor_mul(i, i, r_bc[pp]),
                   waits=(c_mark[pp],) + grp_cvt[grp] if ic == 0 else ())
            last = rec.op("vector", lambda e, o=xsl: e.tensor_sub(o, o, mr_bc[pp]),
                          inc=s["dve"])
        apply_last[pp] = last

    stats(0)
    stats(1)
    rows_bcast(0)
    apply(0)
    stats(2)
    rows_bcast(1)
    apply(1)
    stats(3)
    rows_bcast(2)
    apply(2)
    rows_bcast(3)
    apply(3)
    rec.barrier()


class _NormState:

    def __init__(self, rec, bufs):
        self.rec = rec
        self.bufs = bufs
        self.i = 0
        self.upd = [None] * 4
        self.sq = [bufs.bf[:, BC + 2112 + k * 512:BC + 2624 + k * 512]
                   for k in range(4)]
        self.ones_b = bufs.bf[:, BC + 4160:BC + 4161]
        self.t11 = [bufs.f32[0:1, FS + 7170 + k:FS + 7171 + k] for k in range(4)]
        self.prow = [bufs.ps[6][0:1, :], bufs.ps[7][0:1, :],
                     bufs.ps[6][32:33, :], bufs.ps[7][32:33, :]]
        self.pending = []
        self.ones_rdy = rec.op("vector", lambda e: e.memset(self.ones_b, 1.0),
                               inc=rec.s["dve"])

    def update(self, stage_ap, tgt):
        rec = self.rec
        s = rec.s
        k = self.i % 4
        self.i += 1
        sqa = rec.op("scalar", lambda e, o=self.sq[k]: e.activation(
            out=o, in_=stage_ap, func=AF.Square), inc=s["act"],
            waits=(self.upd[k],))
        self.pending.append((k, tgt, sqa))
        return sqa, sqa

    def flush(self):
        rec, bufs = self.rec, self.bufs
        s = rec.s
        last = None
        for k, tgt, sqa in self.pending:
            nm = rec.op("tensor", lambda e, o=self.sq[k], pk=self.prow[k]:
                        e.matmul(out=pk, lhsT=self.ones_b, rhs=o,
                                 start=True, stop=True),
                        inc=s["te"], waits=(sqa, self.ones_rdy))
            self.ones_rdy = None
            rec.op("vector", lambda e, t=self.t11[k], pk=self.prow[k]:
                   e.tensor_reduce(out=t, in_=pk, axis=AX.X, op=ALU.max),
                   waits=(nm,), fence=True)
            upd = rec.op("vector", lambda e, t=self.t11[k], tg=tgt:
                         e.tensor_max(tg, tg, t), inc=s["dve"])
            self.upd[k] = upd
            last = upd
        self.pending = []
        return last


def _emit_norms_and_gather(rec, bufs, io, L, maxk2, norm_prev, fused):
    s = rec.s
    bf = bufs.bf
    kvl = io.tensors[f"kvl{L}"].ap()
    nrow = KV0_N if L == 0 else KV1_N
    nrow_bf = bf[0:1, BC + 2048:BC + 2112]
    dv = rec.op("vector", lambda e: e.tensor_scalar(
        out=nrow_bf[0:1, 0:32], in0=maxk2, scalar1=1.0404, scalar2=None,
        op0=ALU.mult), inc=s["dve"], waits=(norm_prev,), fence=True)
    nst = rec.dma("sync", kvl[nrow:nrow + 1, 0:64], nrow_bf, waits=(dv,))
    if fused:
        kvg = io.tensors[f"kvg{L}"].ap()
        cc = s[f"cc{L}"]
        waits = [nst] + [(s[f"ss{g}"], s[f"ss{g}"].cnt) for g in range(NGRP)]
        rec.op("gpsimd", lambda e: e.collective_compute(
            "AllGather", ALU.bypass,
            replica_groups=[[0, 1], [2, 3], [4, 5], [6, 7]],
            ins=[kvl], outs=[kvg]), inc=cc, waits=tuple(waits))


def _emit_qn(rec, bufs, io, L, maxq2, norm_prev):
    s = rec.s
    f32 = bufs.f32
    qn = io.tensors[f"qn{L}"].ap()
    qrow = f32[0:1, FS + 480:FS + 512]
    dv = rec.op("vector", lambda e: e.tensor_scalar(
        out=qrow, in0=maxq2, scalar1=1.01, scalar2=None,
        op0=ALU.mult), inc=s["dve"], waits=(norm_prev,))
    rec.dma("sync", qn, qrow, waits=(dv,))


def ph_qkv0(rec, bufs, io, fused=False):
    rec.paranoid = "qkv0" in PARANOID
    s = rec.s
    bf, f32, ps = bufs.bf, bufs.f32, bufs.ps
    wq = io.ap("wqT", (NIC, 128, D), F32, "ExternalInput")
    wk = io.ap("wkT", (NIC, 128, D), F32, "ExternalInput")
    wv = io.ap("wvT", (NIC, 128, D), F32, "ExternalInput")
    qd = io.tensors["qd0"].ap()
    kvl = io.tensors["kvl0"].ap()

    wbuf = [bf[:, BB:BB + 4096], bf[:, BB + 4608:BB + 4608 + 4096]]
    stg = [bf[:, BC + i * 512:BC + (i + 1) * 512] for i in range(4)]
    norms = _NormState(rec, bufs)
    maxq2 = f32[0:1, FP + P_MAXQ2:FP + P_MAXQ2 + 32]
    maxk2 = f32[0:1, FS + 520:FS + 552]
    rec.op("vector", lambda e: e.memset(maxk2, 0.0))
    rec.op("vector", lambda e: e.memset(maxq2, 0.0))

    specs = [(wk, 1, kvl, KV0_K, "k"), (wv, 2, kvl, KV0_V, "v"),
             (wq, 0, qd, 0, "q")]
    wld = [None, None]
    te_done = [None, None]
    st_done = [None] * NGRP
    ps_free = [None] * 6
    sq_guard = [None] * NGRP
    norm_prev = None
    spec_chain = 0
    for w, bidx, dst, dbase, kind in specs:
        if kind == "q":
            _emit_norms_and_gather(rec, bufs, io, 0, maxk2, norm_prev, fused)
        for oc in range(NIC):
            wb = oc % 2
            wld[wb] = rec.dma("gpsimd", wbuf[wb], w[oc],
                              waits=(te_done[wb],), sem=s[f"sw{wb}"])
            mm = [None] * NGRP
            bk = [(spec_chain + grp) % 6 for grp in range(NGRP)]
            for ic in range(NIC):
                for grp in range(NGRP):
                    waits = (wld[wb], ps_free[bk[grp]]) if ic == 0 else ()
                    xsl = bf[:, BA + ic * T + grp * GW: BA + ic * T + (grp + 1) * GW]
                    mm[grp] = rec.op(
                        "tensor",
                        lambda e, wt=wbuf[wb][:, ic * 128:(ic + 1) * 128], xs=xsl,
                        f=(ic == 0), l=(ic == NIC - 1), g=bk[grp]:
                        e.matmul(out=ps[g][:, :], lhsT=wt, rhs=xs, start=f, stop=l),
                        inc=s["te"], waits=waits)
                if ic == 1:
                    norm_prev = norms.flush() or norm_prev
            spec_chain += NGRP
            te_done[wb] = mm[NGRP - 1]
            bias_col = f32[:, FP + P_BQ + bidx * 32 + oc:FP + P_BQ + bidx * 32 + oc + 1]
            for grp in range(NGRP):
                gsl = slice(grp * GW, (grp + 1) * GW)
                ev = rec.op("vector",
                            lambda e, g=bk[grp], o=stg[grp], bc=bias_col:
                            e.tensor_scalar(out=o, in0=ps[g][:, :], scalar1=bc,
                                            scalar2=None, op0=ALU.add),
                            inc=s["dve"], waits=(mm[grp], st_done[grp], sq_guard[grp]))
                ps_free[bk[grp]] = ev
                if kind in ("q", "k"):
                    tgt = maxq2[0:1, oc:oc + 1] if kind == "q" else maxk2[0:1, oc:oc + 1]
                    rec.wait("scalar", ev)
                    _, sq_guard[grp] = norms.update(stg[grp], tgt)
                st_done[grp] = rec.dma(
                    "sync", dst[dbase + oc * 128:dbase + (oc + 1) * 128, gsl],
                    stg[grp], waits=(ev,), sem=s[f"ss{grp}"])
    norm_prev = norms.flush() or norm_prev
    _emit_qn(rec, bufs, io, 0, maxq2, norm_prev)
    rec.barrier()


def ph_qkv1(rec, bufs, io, fused=False):
    rec.paranoid = "qkv1" in PARANOID
    s = rec.s
    bf, f32, ps = bufs.bf, bufs.f32, bufs.ps
    qw = {"qwq": io.ap("qwq", (32, 128, D), I32, "ExternalInput"),
          "qwk": io.ap("qwk", (8, 128, D), I32, "ExternalInput"),
          "qwv": io.ap("qwv", (8, 128, D), I32, "ExternalInput")}
    sc = {"scq": io.t("scq", (32, 4096), F32, "ExternalInput"),
          "sck": io.t("sck", (8, 4096), F32, "ExternalInput"),
          "scv": io.t("scv", (8, 4096), F32, "ExternalInput")}
    qd = io.tensors["qd1"].ap()
    kvl = io.tensors["kvl1"].ap()
    ropec = io.ap("ropec", (128, T), F32, "ExternalInput")
    ropes = io.ap("ropes", (128, T), F32, "ExternalInput")

    wbuf = [bf[:, BB:BB + 4096], bf[:, BB + 4608:BB + 4608 + 4096]]
    scbc = [bf[:, BB + 9216:BB + 9216 + 4096], bf[:, BB + 13312:BB + 13312 + 4096]]
    stg = [bf[:, BC + i * 512:BC + (i + 1) * 512] for i in range(4)]
    cos = f32[:, FS:FS + 2048]
    sin = f32[:, FS + 2048:FS + 4096]
    rb = f32[:, FS + 5120:FS + 5632]
    sw = f32[:, FS + 5632:FS + 6144]
    t1b = f32[:, FS + 4096:FS + 4608]
    norms = _NormState(rec, bufs)
    maxq2 = f32[0:1, FP + P_MAXQ2:FP + P_MAXQ2 + 32]
    maxk2 = f32[0:1, FS + 7180:FS + 7212]

    ldc = rec.dma("sync", cos, ropec)
    lds = rec.dma("sync", sin, ropes)
    rec.op("vector", lambda e: e.memset(maxk2, 0.0), waits=(ldc, lds))
    rec.op("vector", lambda e: e.memset(maxq2, 0.0))

    specs = [("qwk", "sck", 8, kvl, KV1_K, True, "k"),
             ("qwv", "scv", 8, kvl, KV1_V, False, "v"),
             ("qwq", "scq", 32, qd, 0, True, "q")]
    wld = [None, None]
    te_done = [None, None]
    st_done = [None] * NGRP
    ps_free = [None] * 6
    sq_guard = [None] * NGRP
    dq_done = [None, None]
    norm_prev = None
    rope_chain = None
    spec_chain = 0
    pend_evac = None

    def emit_evac(pe):
        nonlocal rope_chain, norm_prev
        oc, mm, bkl, dst_, dbase_, rope_, kind_ = pe
        for grp in range(NGRP):
            gsl = slice(grp * GW, (grp + 1) * GW)
            if rope_:
                ev0 = rec.op("scalar",
                             lambda e, g=bkl[grp]: e.activation(
                                 out=rb, in_=ps[g][:, :], func=AF.Copy),
                             inc=s["act"],
                             waits=(mm[grp], rope_chain))
                ps_free[bkl[grp]] = ev0
                cs = cos[:, grp * GW:(grp + 1) * GW]
                ss = sin[:, grp * GW:(grp + 1) * GW]
                rec.op("vector", lambda e, c_=cs: e.tensor_mul(t1b, rb, c_),
                       waits=(ev0,))
                rec.op("vector", lambda e, s_=ss: e.tensor_mul(
                    sw[0:64, :], rb[64:128, :], s_[64:128, :]))
                rec.op("vector", lambda e, s_=ss: e.tensor_mul(
                    sw[64:128, :], rb[0:64, :], s_[0:64, :]))
                evm = rec.op("vector",
                             lambda e, o=stg[grp]: e.tensor_add(o, t1b, sw),
                             inc=s["dve"], waits=(st_done[grp], sq_guard[grp]))
                rope_chain = evm
            else:
                evm = rec.op("scalar",
                             lambda e, g=bkl[grp], o=stg[grp]:
                             e.activation(out=o, in_=ps[g][:, :], func=AF.Copy),
                             inc=s["act"],
                             waits=(mm[grp], st_done[grp], sq_guard[grp]))
                ps_free[bkl[grp]] = evm
            if kind_ in ("q", "k"):
                tgt = maxq2[0:1, oc:oc + 1] if kind_ == "q" else maxk2[0:1, oc:oc + 1]
                rec.wait("scalar", evm)
                _, sq_guard[grp] = norms.update(stg[grp], tgt)
            st_done[grp] = rec.dma(
                "sync", dst_[dbase_ + oc * 128:dbase_ + (oc + 1) * 128, gsl],
                stg[grp], waits=(evm,), sem=s[f"ss{grp}"])

    for wn, scn, nout, dst, dbase, rope, kind in specs:
        if kind == "q":
            if pend_evac is not None:
                emit_evac(pend_evac)
                pend_evac = None
            norm_prev = norms.flush() or norm_prev
            _emit_norms_and_gather(rec, bufs, io, 1, maxk2, norm_prev, fused)
        w_ap = qw[wn]
        for oc in range(nout):
            wb = oc % 2
            wld[wb] = rec.dma("gpsimd", wbuf[wb], w_ap[oc],
                              waits=(te_done[wb],), sem=s[f"sw{wb}"])
            h0 = rec.dma("gpsimd", scbc[wb][0:16, :],
                         _AP(sc[scn].ap().tensor, oc * 4096,
                             [[0, 16], [1, 4096]]),
                         waits=(dq_done[wb],), sem=s[f"sc{wb}"])
            h1 = rec.dma("scalar", scbc[wb][16:32, :], scbc[wb][0:16, :],
                         waits=(h0,), sem=s[f"sc{wb}"])
            h2 = rec.dma("scalar", scbc[wb][32:64, :], scbc[wb][0:32, :],
                         waits=(h1,), sem=s[f"sc{wb}"])
            scld = rec.dma("scalar", scbc[wb][64:128, :], scbc[wb][0:64, :],
                           waits=(h2,), sem=s[f"sc{wb}"])
            dq_done[wb] = rec.op(
                "vector", lambda e, wv=wbuf[wb], sv=scbc[wb]:
                e.scalar_tensor_tensor(out=wv, in0=wv, scalar=-8.0, in1=sv,
                                       op0=ALU.add, op1=ALU.mult),
                inc=s["dve"], waits=(wld[wb], scld, te_done[wb]))
            if pend_evac is not None:
                emit_evac(pend_evac)
            mm = [None] * NGRP
            bk = [(spec_chain + grp) % 6 for grp in range(NGRP)]
            for ic in range(NIC):
                for grp in range(NGRP):
                    waits = (dq_done[wb], ps_free[bk[grp]]) if ic == 0 else ()
                    xsl = bf[:, BA + ic * T + grp * GW: BA + ic * T + (grp + 1) * GW]
                    mm[grp] = rec.op(
                        "tensor",
                        lambda e, wt=wbuf[wb][:, ic * 128:(ic + 1) * 128], xs=xsl,
                        f=(ic == 0), l=(ic == NIC - 1), g=bk[grp]:
                        e.matmul(out=ps[g][:, :], lhsT=wt, rhs=xs, start=f, stop=l),
                        inc=s["te"], waits=waits)
                if ic == 1:
                    norm_prev = norms.flush() or norm_prev
            spec_chain += NGRP
            te_done[wb] = mm[NGRP - 1]
            pend_evac = (oc, mm, bk, dst, dbase, rope, kind)
    emit_evac(pend_evac)
    norm_prev = norms.flush() or norm_prev
    _emit_qn(rec, bufs, io, 1, maxq2, norm_prev)
    rec.barrier()


def ph_gather(rec, bufs, io, L):
    s = rec.s
    rows = KV0_ROWS if L == 0 else KV1_ROWS
    kvl = io.tensors[f"kvl{L}"].ap()
    kvg = io.tensors[f"kvg{L}"].ap()
    cc = s[f"cc{L}"]
    rec.op("gpsimd", lambda e: e.collective_compute(
        "AllGather", ALU.bypass, replica_groups=[[0, 1], [2, 3], [4, 5], [6, 7]],
        ins=[kvl], outs=[kvg]), inc=cc)
    for eng in Rec.ENGINES:
        rec.wait(eng, (cc, cc.cnt))
    rec.barrier()


def ph_attn(rec, bufs, io, L, fused=False):
    rec.paranoid = "attn" in PARANOID
    s = rec.s
    bf, f32, ps = bufs.bf, bufs.f32, bufs.ps
    rows = KV0_ROWS if L == 0 else KV1_ROWS
    kbase = KV0_K if L == 0 else KV1_K
    vbase = KV0_V if L == 0 else KV1_V
    nrow = KV0_N if L == 0 else KV1_N
    nkv = NH if L == 0 else NKV
    qpk = 1 if L == 0 else 4
    qd = io.tensors[f"qd{L}"].ap()
    kvg = io.tensors[f"kvg{L}"].ap()
    qn = io.tensors[f"qn{L}"].ap()
    masks_in = io.ap("masks", (8, 128, 512), BF16, "ExternalInput")

    ksb = [bf[:, BB:BB + 4096], bf[:, BB + 4096:BB + 8192]]
    vsb = [bf[:, BB + 8192:BB + 12288], bf[:, BB + 12288:BB + 16384]]
    masks = bf[:, BC:BC + 4096]
    qt = [bf[:, BC + 4096 + i * 512:BC + 4096 + (i + 1) * 512] for i in range(2)]
    pt = [bf[:, BC + 5120 + i * 512:BC + 5120 + (i + 1) * 512] for i in range(2)] + \
         [bf[:, BB + 16384 + i * 512:BB + 16896 + i * 512] for i in range(2)]
    ones_bf = masks[:, 511:512]
    k2a = f32[0:1, FS:FS + 32]
    k2b = f32[0:1, FS + 32:FS + 64]
    q2a = f32[0:1, FS + 64:FS + 96]
    lnq = f32[0:1, FS + 128:FS + 160]
    brow = f32[0:1, FS + 160:FS + 192]
    rrow = f32[0:1, FS + 192:FS + 704]
    rbc = f32[:, FS + 704:FS + 1216]
    biasall = f32[:, FS + 1216:FS + 1248]
    ones1p = f32[0:1, FP + P_ONES1P:FP + P_ONES1P + 128]

    ccm = (s[f"cc{L}"], 1) if fused else None
    mld = rec.dma("sync", masks, masks_in.rearrange("m p c -> p m c"))
    onesb_rdy = mld
    rec.wait("gpsimd", ccm)
    rec.dma("gpsimd", k2a, kvg[nrow:nrow + 1, 0:32])
    m2 = rec.dma("gpsimd", k2b, kvg[rows + nrow:rows + nrow + 1, 0:32])
    m3 = rec.dma("gpsimd", q2a, qn)
    d0 = rec.op("vector", lambda e: e.tensor_max(k2a, k2a, k2b), inc=s["dve"],
                waits=(m2, m3))
    l1 = rec.op("scalar", lambda e: e.activation(out=lnq, in_=q2a, func=AF.Ln),
                inc=s["act"], waits=(d0,))
    l2 = rec.op("scalar", lambda e: e.activation(out=k2b, in_=k2a, func=AF.Ln),
                inc=s["act"])
    if qpk == 1:
        ksrc = k2b
    else:
        ksrc = _AP(k2b.tensor, k2b.offset, [k2b.ap[0], [1, 8], [0, 4]])
    d1 = rec.op("vector", lambda e, ks=ksrc: e.scalar_tensor_tensor(
        out=brow, in0=lnq, scalar=-math.log(HD), in1=ks,
        op0=ALU.add, op1=ALU.add), inc=s["dve"], waits=(l1, l2))
    e1 = rec.op("scalar", lambda e: e.activation(
        out=brow, in_=brow, func=AF.Exp, scale=0.5),
        inc=s["act"], waits=(d1,))
    negb = rec.op("vector", lambda e: e.tensor_scalar(
        out=brow, in0=brow, scalar1=-1.0, scalar2=None, op0=ALU.mult),
        inc=s["dve"], waits=(e1,))
    bm = rec.op("tensor", lambda e: e.matmul(
        out=ps[5][:, 0:32], lhsT=ones1p, rhs=brow[0:1, 0:32],
        start=True, stop=True), inc=s["te"], waits=(negb,))
    bias_rdy = rec.op("vector", lambda e: e.tensor_copy(
        out=biasall, in_=ps[5][:, 0:32]), inc=s["dve"], waits=(bm,))
    if DBG_ATTN:
        dbg_b = io.ap(f"dbg_brow{L}", (1, 192), F32, "ExternalOutput")
        rec.dma("sync", dbg_b[0:1, 0:32], brow, waits=(negb,))

    kld = [None, None]
    vld = [None, None]
    te_free = [None, None]

    def load_kv(kvh, kb):
        sem = s[f"skv{kb}"]
        w0 = (te_free[kb], ccm)
        for r in range(2):
            dview = ksb[kb].rearrange("p (t a o) -> p a t o", a=2, o=128)[:, r, :, :]
            srow = r * rows + kbase + kvh * 128
            rec.dma("sync", dview, kvg[srow:srow + 128, :],
                    waits=w0 if r == 0 else (), sem=sem)
        lv = None
        for j in range(32):
            r, loc = j % 2, j // 2
            srow = r * rows + vbase + kvh * 128
            lv = rec.dma("sync", vsb[kb][:, j * 128:(j + 1) * 128],
                         kvg[srow:srow + 128, loc * 128:(loc + 1) * 128], tr=True,
                         sem=sem)
        return lv

    kld[0] = vld[0] = load_kv(0, 0)
    if nkv > 1:
        kld[1] = vld[1] = load_kv(1, 1)

    seq = [(kvh * qpk + sub, grp)
           for kvh in range(nkv) for sub in range(qpk) for grp in range(NGRP)]

    qmark = [None, None]
    qused = [None, None]

    def load_q(i):
        h, grp = seq[i]
        qb = i % 2
        qmark[qb] = rec.dma("scalar", qt[qb],
                            qd[h * 128:(h + 1) * 128, grp * GW:(grp + 1) * GW],
                            waits=(qused[qb],), sem=s[f"sqt{qb}"])

    load_q(0)
    load_q(1)

    exp_done = [None] * 4
    av_done = [None] * 4
    rc_act = None
    prev_rbm = None
    prev_recip = None
    ctx_free = None
    pend = []
    epi = None

    def emit_av(p):
        av = rec.op(
            "tensor",
            lambda e, vt=vsb[p["kb"]][:, p["j"] * 128:(p["j"] + 1) * 128],
            pp=pt[p["pb"]], f=p["first"], l=p["last"]:
            e.matmul(out=ps[4][:, :], lhsT=vt, rhs=pp, start=f, stop=l),
            inc=s["te"],
            waits=(p["dep"], p["vld"], p["ctx_free"] if p["first"] else None))
        return av

    def emit_den(p):
        den = rec.op(
            "tensor",
            lambda e, pp=pt[p["pb"]], f=p["first"], l=p["last"]:
            e.matmul(out=ps[6][0:1, :], lhsT=ones_bf, rhs=pp, start=f, stop=l),
            inc=s["te"],
            waits=(p["rc_act"] if p["first"] else None, p["ones"]))
        av_done[p["pb"]] = den
        return den

    def emit_epi(pe):
        nonlocal prev_rbm, prev_recip, ctx_free
        rbm = rec.op("tensor", lambda e: e.matmul(
            out=ps[7][:, :], lhsT=ones1p, rhs=rrow, start=True, stop=True),
            inc=s["te"], waits=(pe["rca"], prev_recip))
        prev_rbm = rbm
        rc_ = rec.op("vector", lambda e: e.reciprocal(rbc, ps[7][:, :]),
                     inc=s["dve"], waits=(rbm,))
        prev_recip = rc_
        ctx_sl = bf[:, BA + pe["h"] * T + pe["grp"] * GW:
                    BA + pe["h"] * T + (pe["grp"] + 1) * GW]
        ctx_free = rec.op("vector",
                          lambda e, o=ctx_sl: e.tensor_mul(o, ps[4][:, :], rbc),
                          inc=s["dve"], waits=(rc_,))

    for i, (h, grp) in enumerate(seq):
        kb = (h // qpk) % 2
        qb = i % 2
        jmax = 8 * grp + 8
        last_mm = None
        for j in range(jmax):
            sb = pb = j % 4
            mm_s = rec.op(
                "tensor",
                lambda e, kt=ksb[kb][:, j * 128:(j + 1) * 128], q=qt[qb], b=sb:
                e.matmul(out=ps[b][:, :], lhsT=kt, rhs=q, start=True, stop=True),
                inc=s["te"],
                waits=(kld[kb], qmark[qb], exp_done[sb]))
            last_mm = mm_s
            if j == 0 and epi is not None:
                emit_epi(epi)
                epi = None
            ex = rec.op(
                "scalar",
                lambda e, b=sb, p=pt[pb], hh=h:
                e.activation(out=p, in_=ps[b][:, :], func=AF.Exp,
                             bias=biasall[:, hh:hh + 1], scale=ISQ),
                inc=s["act"], waits=(mm_s, av_done[pb], bias_rdy))
            bias_rdy = None
            exp_done[sb] = ex
            m = j - 8 * grp
            if m >= 0:
                dep = rec.op(
                    "vector",
                    lambda e, p=pt[pb], mk=masks[:, m * 512:(m + 1) * 512]:
                    e.tensor_mul(p, p, mk), inc=s["dve"],
                    waits=(ex, mld))
            else:
                dep = ex
            if len(pend) == 2:
                p0 = pend.pop(0)
                emit_av(p0)
                emit_den(p0)
            pend.append(dict(j=j, kb=kb, pb=pb, dep=dep, first=(j == 0),
                             last=(j == jmax - 1), vld=vld[kb],
                             ctx_free=ctx_free, rc_act=rc_act,
                             ones=onesb_rdy))
            onesb_rdy = None
        qused[qb] = last_mm
        if i + 2 < len(seq):
            load_q(i + 2)
        for p0 in pend:
            emit_av(p0)
            emit_den(p0)
        pend = []
        rca = rec.op("scalar", lambda e: e.activation(
            out=rrow, in_=ps[6][0:1, :], func=AF.Copy), inc=s["act"],
            waits=((s["te"], s["te"].cnt), prev_rbm))
        rc_act = rca
        epi = dict(h=h, grp=grp, rca=rca)
        if grp == NGRP - 1 and h % qpk == qpk - 1:
            hkv = h // qpk
            if hkv + 2 < nkv:
                te_free[kb] = (s["te"], s["te"].cnt)
                kld[kb] = vld[kb] = load_kv(hkv + 2, kb)
    emit_epi(epi)
    epi = None
    rec.barrier()



def ph_oproj(rec, bufs, io, L):
    rec.paranoid = "oproj" in PARANOID
    s = rec.s
    bf, f32, ps = bufs.bf, bufs.f32, bufs.ps
    x_in = io.tensors[f"x{2 * L}"].ap()
    x_out = io.tensors[f"x{2 * L + 1}"].ap()
    wbuf = [bf[:, BB:BB + 4096], bf[:, BB + 4608:BB + 4608 + 4096]]
    scbc = [bf[:, BB + 9216:BB + 9216 + 4096], bf[:, BB + 13312:BB + 13312 + 4096]]
    xold = [f32[:, FS:FS + 512], f32[:, FS + 512:FS + 1024]]
    stgf = [f32[:, FS + 1024:FS + 1536], f32[:, FS + 1536:FS + 2048]]

    if L == 0:
        w_ap = io.ap("woT", (NIC, 128, D), F32, "ExternalInput")
        sc_h = None
    else:
        w_ap = io.ap("qwo", (NIC, 128, D), I32, "ExternalInput")
        sc_h = io.t("sco", (32, 4096), F32, "ExternalInput")

    wld = [None, None]
    te_done = [None, None]
    dq_done = [None, None]
    st_done = [None, None]
    ev_done = [None, None]
    ps_free = [None] * 6
    xld = [None, None]
    pend_evac = None

    def emit_evac(pe):
        oc, mm, bkl = pe
        for grp in range(NGRP):
            b = grp % 2
            gsl = slice(grp * GW, (grp + 1) * GW)
            xld[b] = rec.dma("scalar", xold[b], x_in[oc * 128:(oc + 1) * 128, gsl],
                             waits=(ev_done[b],), sem=s[f"sx{b}"])
            if L == 0:
                bias_col = f32[:, FP + P_BQ + 3 * 32 + oc:FP + P_BQ + 3 * 32 + oc + 1]
                ev = rec.op("vector",
                            lambda e, g=bkl[grp], o=stgf[b], xo=xold[b], bc=bias_col:
                            e.scalar_tensor_tensor(out=o, in0=ps[g][:, :], scalar=bc,
                                                   in1=xo, op0=ALU.add, op1=ALU.add),
                            inc=s["dve"], waits=(mm[grp], xld[b], st_done[b]))
            else:
                ev = rec.op("vector",
                            lambda e, g=bkl[grp], o=stgf[b], xo=xold[b]:
                            e.tensor_add(o, ps[g][:, :], xo),
                            inc=s["dve"], waits=(mm[grp], xld[b], st_done[b]))
            ev_done[b] = ev
            ps_free[bkl[grp]] = ev
            st_done[b] = rec.dma("sync", x_out[oc * 128:(oc + 1) * 128, gsl],
                                 stgf[b], waits=(ev,), sem=s[f"ss{b}"])

    for oc in range(NIC):
        wb = oc % 2
        wld[wb] = rec.dma("gpsimd", wbuf[wb], w_ap[oc],
                          waits=(te_done[wb],), sem=s[f"sw{wb}"])
        if L == 1:
            h0 = rec.dma("gpsimd", scbc[wb][0:16, :],
                         _AP(sc_h.ap().tensor, oc * 4096,
                             [[0, 16], [1, 4096]]),
                         waits=(dq_done[wb],), sem=s[f"sc{wb}"])
            h1 = rec.dma("scalar", scbc[wb][16:32, :], scbc[wb][0:16, :],
                         waits=(h0,), sem=s[f"sc{wb}"])
            h2 = rec.dma("scalar", scbc[wb][32:64, :], scbc[wb][0:32, :],
                         waits=(h1,), sem=s[f"sc{wb}"])
            scld = rec.dma("scalar", scbc[wb][64:128, :], scbc[wb][0:64, :],
                           waits=(h2,), sem=s[f"sc{wb}"])
            dq_done[wb] = rec.op(
                "vector", lambda e, wv=wbuf[wb], sv=scbc[wb]:
                e.scalar_tensor_tensor(out=wv, in0=wv, scalar=-8.0, in1=sv,
                                       op0=ALU.add, op1=ALU.mult),
                inc=s["dve"], waits=(wld[wb], scld, te_done[wb]))
            wrdy = dq_done[wb]
        else:
            wrdy = wld[wb]
        if pend_evac is not None:
            emit_evac(pend_evac)
        mm = [None] * NGRP
        bk = [(oc * NGRP + grp) % 6 for grp in range(NGRP)]
        for ic in range(NIC):
            for grp in range(NGRP):
                waits = (wrdy, ps_free[bk[grp]]) if ic == 0 else ()
                csl = bf[:, BA + ic * T + grp * GW: BA + ic * T + (grp + 1) * GW]
                mm[grp] = rec.op(
                    "tensor",
                    lambda e, wt=wbuf[wb][:, ic * 128:(ic + 1) * 128], cs=csl,
                    f=(ic == 0), l=(ic == NIC - 1), g=bk[grp]:
                    e.matmul(out=ps[g][:, :], lhsT=wt, rhs=cs, start=f, stop=l),
                    inc=s["te"], waits=waits)
        te_done[wb] = mm[NGRP - 1]
        pend_evac = (oc, mm, bk)
    emit_evac(pend_evac)
    rec.barrier()


def ph_ffn0(rec, bufs, io):
    rec.paranoid = "ffn0" in PARANOID
    s = rec.s
    bf, f32, ps = bufs.bf, bufs.f32, bufs.ps
    x_in = io.tensors["x1"].ap()
    x_out = io.tensors["x2"].ap()
    gw_ap = io.ap("gatewT", (D, 28), F32, "ExternalInput")
    a2_ap = io.ap("a2T", (D, 224), F32, "ExternalInput")
    b2_ap = io.ap("b2", (224, D), F32, "ExternalInput")
    bw_ap = io.ap("basewT", (NIC, 128, D), F32, "ExternalInput")
    ex_ap = io.ap("ex28", (28, 224), F32, "ExternalInput")

    wbuf = [bf[:, BB:BB + 4608], bf[:, BB + 4608:BB + 9216]]
    hg = bf[:, BB + 9216:BB + 13312]
    gexs = [f32[:, FS + 5120:FS + 5632], f32[:, FS + 5632:FS + 6144]]
    logits = f32[:, FS + 2048:FS + 2048 + 28]
    m8 = f32[:, FS + 2080:FS + 2088]
    srow = f32[:, FS + 2088:FS + 2096]
    gts = f32[:, FS + 2104:FS + 2132]
    gT = f32[0:28, FS + 2432:FS + 2432 + 2048]
    exp28 = f32[0:28, FS + 2144:FS + 2144 + 224]
    xold = [f32[:, FS:FS + 512], f32[:, FS + 512:FS + 1024]]
    stgf = [f32[:, FS + 1024:FS + 1536], f32[:, FS + 1536:FS + 2048]]
    iden = f32[:, FP + P_IDEN:FP + P_IDEN + 128]
    gwsb = bf[:, BC + 2048:BC + 2048 + 896]

    exld = rec.dma("sync", exp28, ex_ap)
    gwld = rec.dma("gpsimd", gwsb.rearrange("p (ic e) -> p ic e", e=28),
                   gw_ap.rearrange("(ic p) e -> p ic e", p=128))

    prev_lg = None
    prev_gT = None
    prev_tr = None
    for tt in range(NT):
        mm = None
        for ic in range(NIC):
            xsl = bf[:, BA + ic * T + tt * 128: BA + ic * T + (tt + 1) * 128]
            waits = (gwld, prev_lg) if ic == 0 else ()
            mm = rec.op("tensor",
                        lambda e, wt=gwsb[:, ic * 28:(ic + 1) * 28], xs=xsl,
                        f=(ic == 0), l=(ic == NIC - 1):
                        e.matmul(out=ps[0][0:128, 0:28], lhsT=xs, rhs=wt,
                                 start=f, stop=l),
                        inc=s["te"], waits=waits)
        prev_lg = rec.op("vector", lambda e: e.tensor_copy(out=logits, in_=ps[0][:, 0:28]),
                         inc=s["dve"], waits=(mm,), fence=True)
        rec.op("vector", lambda e: e.max(out=m8, in_=logits), fence=True)
        rec.op("vector", lambda e: e.tensor_sub(srow[:, 0:1], m8[:, 1:2], m8[:, 0:1]),
               fence=True)
        dv = rec.op("vector", lambda e: e.tensor_copy(out=srow[:, 1:2], in_=srow[:, 0:1]),
                    inc=s["dve"], fence=True)
        ea = rec.op("scalar", lambda e: e.activation(
            out=srow[:, 1:2], in_=srow[:, 1:2], func=AF.Exp), inc=s["act"], waits=(dv,))
        rec.op("vector", lambda e: e.tensor_scalar(
            out=srow[:, 2:3], in0=srow[:, 1:2], scalar1=1.0, scalar2=None,
            op0=ALU.add), waits=(ea,), fence=True)
        rec.op("vector", lambda e: e.reciprocal(srow[:, 3:4], srow[:, 2:3]),
               fence=True)
        rec.op("vector", lambda e: e.tensor_mul(srow[:, 4:5], srow[:, 1:2], srow[:, 3:4]),
               fence=True)
        rec.op("vector", lambda e: e.tensor_scalar(
            out=gts, in0=logits, scalar1=m8[:, 0:1], scalar2=srow[:, 3:4],
            op0=ALU.is_equal, op1=ALU.mult), waits=(prev_tr,), fence=True)
        rec.op("vector", lambda e: e.tensor_scalar(
            out=logits, in0=logits, scalar1=m8[:, 1:2], scalar2=srow[:, 4:5],
            op0=ALU.is_equal, op1=ALU.mult), fence=True)
        gd = rec.op("vector", lambda e: e.tensor_add(gts, gts, logits),
                    inc=s["dve"])
        tr = rec.op("tensor", lambda e: e.transpose(ps[1][0:28, 0:128], gts, iden),
                    inc=s["te"], waits=(gd, prev_gT))
        prev_tr = tr
        prev_gT = rec.op("vector", lambda e, c=tt: e.tensor_copy(
            out=gT[:, c * 128:(c + 1) * 128], in_=ps[1][0:28, 0:128]),
            inc=s["dve"], waits=(tr,))
    rec.barrier()
    if DBG_FFN:
        dbg_g = io.ap("dbg_gT0", (28, 2048), F32, "ExternalOutput")
        rec.dma("sync", dbg_g, gT)
        rec.barrier()

    ew = [128, 96]
    wldA = [None, None]
    te_dA = [None, None]
    hg_done = {}
    prev_ge = [None, None]
    gex_free = [None, None]
    for ec in range(2):
        wb = ec % 2
        src = (_wtile_src(a2_ap, 0) if ec == 0 else
               a2_ap.rearrange("(ic p) o -> p ic o", p=128)[:, :, 128:224])
        wldA[wb] = rec.dma("gpsimd", wbuf[wb][:, 0:NIC * ew[ec]], src,
                           waits=(te_dA[wb],), sem=s[f"sw{wb}"])
        mm = [None] * NGRP
        for ic in range(NIC):
            for grp in range(NGRP):
                waits = ((wldA[wb], hg_done.get((ec - 1, grp)))
                         if ic == 0 else ())
                xsl = bf[:, BA + ic * T + grp * GW: BA + ic * T + (grp + 1) * GW]
                mm[grp] = rec.op(
                    "tensor",
                    lambda e, wt=wbuf[wb][:, ic * ew[ec]:(ic + 1) * ew[ec]], xs=xsl,
                    f=(ic == 0), l=(ic == NIC - 1), g=grp, w=ew[ec]:
                    e.matmul(out=ps[g][0:w, :], lhsT=wt, rhs=xs, start=f, stop=l),
                    inc=s["te"], waits=waits)
        te_dA[wb] = mm[NGRP - 1]
        for grp in range(NGRP):
            b = grp % 2
            gx = rec.op("tensor",
                        lambda e, g=grp, c=ec, b_=b: e.matmul(
                            out=ps[6 + b_][0:ew[c], :],
                            lhsT=exp28[:, c * 128:c * 128 + ew[c]],
                            rhs=gT[:, g * GW:(g + 1) * GW],
                            start=True, stop=True),
                        inc=s["te"], waits=(mm[grp], exld, prev_ge[b]))
            ge = rec.op("scalar", lambda e, b_=b, c=ec: e.activation(
                out=gexs[b_][0:ew[c], :], in_=ps[6 + b_][0:ew[c], :], func=AF.Copy),
                inc=s["act"], waits=(gx, gex_free[b]))
            prev_ge[b] = ge
            hgm = rec.op("vector",
                         lambda e, g=grp, c=ec, b_=b,
                         o=hg[0:ew[ec], ec * T + grp * GW:ec * T + (grp + 1) * GW]:
                         e.tensor_mul(o, ps[g][0:ew[c], :], gexs[b_][0:ew[c], :]),
                         inc=s["dve"], waits=(ge,))
            hg_done[(ec, grp)] = hgm
            gex_free[b] = hgm

    wld = [None, None]
    te_done = [te_dA[0], te_dA[1]]
    st_done = [None, None]
    ev_done = [None, None]
    ps_free = [None] * 6
    for g in range(NGRP):
        ps_free[g] = hg_done[(1, g)]
    xld = [None, None]
    for oc in range(NIC):
        wb = oc % 2
        rec.dma("gpsimd", wbuf[wb][:, 0:4096], bw_ap[oc],
                waits=(te_done[wb],), sem=s[f"sw{wb}"])
        rec.dma("gpsimd", wbuf[wb][0:128, 4096:4224],
                b2_ap[0:128, oc * 128:(oc + 1) * 128], sem=s[f"sw{wb}"])
        wld[wb] = rec.dma("gpsimd", wbuf[wb][0:96, 4224:4352],
                          b2_ap[128:224, oc * 128:(oc + 1) * 128], sem=s[f"sw{wb}"])
        mm = [None] * NGRP
        bk = [(oc * NGRP + grp) % 6 for grp in range(NGRP)]
        for ic in range(NIC):
            for grp in range(NGRP):
                waits = (wld[wb], ps_free[bk[grp]]) if ic == 0 else ()
                xsl = bf[:, BA + ic * T + grp * GW: BA + ic * T + (grp + 1) * GW]
                mm[grp] = rec.op(
                    "tensor",
                    lambda e, wt=wbuf[wb][:, ic * 128:(ic + 1) * 128], xs=xsl,
                    f=(ic == 0), g=bk[grp]:
                    e.matmul(out=ps[g][:, :], lhsT=wt, rhs=xs, start=f, stop=False),
                    inc=s["te"], waits=waits)
        for ec in range(2):
            for grp in range(NGRP):
                hsl = hg[0:ew[ec], ec * T + grp * GW:ec * T + (grp + 1) * GW]
                mm[grp] = rec.op(
                    "tensor",
                    lambda e,
                    wt=wbuf[wb][0:ew[ec], 4096 + ec * 128:4224 + ec * 128],
                    hs=hsl, l=(ec == 1), g=bk[grp]:
                    e.matmul(out=ps[g][:, :], lhsT=wt, rhs=hs, start=False, stop=l),
                    inc=s["te"],
                    waits=(hg_done[(ec, grp)],) if oc == 0 else ())
        te_done[wb] = mm[NGRP - 1]
        for grp in range(NGRP):
            b = grp % 2
            gsl = slice(grp * GW, (grp + 1) * GW)
            xld[b] = rec.dma("scalar", xold[b], x_in[oc * 128:(oc + 1) * 128, gsl],
                             waits=(ev_done[b],), sem=s[f"sx{b}"])
            ev = rec.op("vector",
                        lambda e, g=bk[grp], o=stgf[b], xo=xold[b]:
                        e.tensor_add(o, ps[g][:, :], xo),
                        inc=s["dve"], waits=(mm[grp], xld[b], st_done[b]))
            ev_done[b] = ev
            ps_free[bk[grp]] = ev
            st_done[b] = rec.dma("sync", x_out[oc * 128:(oc + 1) * 128, gsl],
                                 stgf[b], waits=(ev,), sem=s[f"ss{b}"])
    rec.barrier()


def ph_ffn1(rec, bufs, io):
    rec.paranoid = "ffn1" in PARANOID
    s = rec.s
    bf, f32, ps = bufs.bf, bufs.f32, bufs.ps
    x_in = io.tensors["x3"].ap()
    x_out = io.tensors["x4"].ap()
    a1_ap = io.ap("a1T", (4, 128, D), F32, "ExternalInput")
    b1_ap = io.ap("b1T", (512, D), F32, "ExternalInput")
    wq_ap = io.ap("wqwT", (D, 16), F32, "ExternalInput")
    keys_h = io.t("keysT", (16, 64), F32, "ExternalInput")
    ex_ap = io.ap("ex64", (64, 512), F32, "ExternalInput")

    wbuf = [bf[:, BB:BB + 4608], bf[:, BB + 4608:BB + 9216]]
    hg = bf[:, BB + 9216:BB + 9216 + 8192]
    gexs = [f32[:, FS + 5120:FS + 5632], f32[:, FS + 5632:FS + 6144]]
    sc64 = f32[:, FS + 2048:FS + 2112]
    e64 = f32[:, FS + 2112:FS + 2176]
    m8a = f32[:, FS + 2176:FS + 2184]
    m8b = f32[:, FS + 2184:FS + 2192]
    srow = f32[:, FS + 2192:FS + 2200]
    gT = f32[0:64, FS + 2432:FS + 2432 + 2048]
    exp64 = f32[0:64, FS + 6144:FS + 6144 + 512]
    qfm = bf[0:16, BC + 2048:BC + 2048 + 2048]
    keysb = bf[0:16, BC + 4096:BC + 4096 + 64]
    wqsb = bf[:, BC + 4160:BC + 4160 + 512]
    xold = [f32[:, FS:FS + 512], f32[:, FS + 512:FS + 1024]]
    stgf = [f32[:, FS + 1024:FS + 1536], f32[:, FS + 1536:FS + 2048]]
    iden = f32[:, FP + P_IDEN:FP + P_IDEN + 128]

    exld = rec.dma("sync", exp64, ex_ap)
    kld = rec.dma("gpsimd", keysb, keys_h.ap())
    wqld = rec.dma("gpsimd", wqsb.rearrange("p (ic r) -> p ic r", r=16),
                   wq_ap.rearrange("(ic p) r -> p ic r", p=128))

    prev_qe = None
    for grp in range(NGRP):
        mm = None
        for ic in range(NIC):
            xsl = bf[:, BA + ic * T + grp * GW: BA + ic * T + (grp + 1) * GW]
            waits = (wqld, prev_qe) if ic == 0 else ()
            mm = rec.op("tensor",
                        lambda e, wt=wqsb[:, ic * 16:(ic + 1) * 16], xs=xsl,
                        f=(ic == 0), l=(ic == NIC - 1):
                        e.matmul(out=ps[0][0:16, :], lhsT=wt, rhs=xs, start=f, stop=l),
                        inc=s["te"], waits=waits)
        prev_qe = rec.op("scalar", lambda e, g=grp: e.activation(
            out=qfm[:, g * GW:(g + 1) * GW], in_=ps[0][0:16, :], func=AF.Copy),
            inc=s["act"], waits=(mm,))
    rec.barrier()
    prev_lg = None
    prev_gT = None
    prev_tr = None
    for tt in range(NT):
        mm = rec.op("tensor", lambda e, c=tt: e.matmul(
            out=ps[0][0:128, 0:64], lhsT=qfm[:, c * 128:(c + 1) * 128], rhs=keysb,
            start=True, stop=True), inc=s["te"],
            waits=((wqld,) if tt == 0 else ()) + ((prev_lg,) if prev_lg else ()))
        prev_lg = rec.op("vector", lambda e: e.tensor_copy(out=sc64, in_=ps[0][:, 0:64]),
                         inc=s["dve"], waits=(mm, prev_tr), fence=True)
        rec.op("vector", lambda e: e.max(out=m8a, in_=sc64), fence=True)
        rec.op("vector", lambda e: e.match_replace(
            out=e64, in_to_replace=m8a, in_values=sc64, imm_value=-1e30),
            fence=True)
        rec.op("vector", lambda e: e.max(out=m8b, in_=e64), fence=True)
        rec.op("vector", lambda e: e.tensor_scalar(
            out=srow[:, 0:1], in0=m8a[:, 0:1], scalar1=-1.0, scalar2=None,
            op0=ALU.mult), fence=True)
        dv = rec.op("vector", lambda e: e.tensor_scalar(
            out=e64, in0=sc64, scalar1=m8b[:, 7:8], scalar2=None, op0=ALU.is_ge),
            inc=s["dve"], fence=True)
        ea = rec.op("scalar", lambda e: e.activation(
            out=sc64, in_=sc64, func=AF.Exp, bias=srow[:, 0:1]), inc=s["act"],
            waits=(dv,))
        rec.op("vector", lambda e: e.tensor_mul(sc64, sc64, e64), waits=(ea,),
               fence=True)
        rec.op("vector", lambda e: e.reduce_sum(srow[:, 1:2], sc64, axis=AX.X),
               fence=True)
        rec.op("vector", lambda e: e.reciprocal(srow[:, 2:3], srow[:, 1:2]),
               fence=True)
        gd = rec.op("vector", lambda e: e.tensor_scalar(
            out=sc64, in0=sc64, scalar1=srow[:, 2:3], scalar2=None, op0=ALU.mult),
            inc=s["dve"])
        tr = rec.op("tensor", lambda e: e.transpose(ps[1][0:64, 0:128], sc64, iden),
                    inc=s["te"], waits=(gd, prev_gT))
        prev_tr = tr
        prev_gT = rec.op("vector", lambda e, c=tt: e.tensor_copy(
            out=gT[:, c * 128:(c + 1) * 128], in_=ps[1][0:64, 0:128]),
            inc=s["dve"], waits=(tr,))
    rec.barrier()

    wldA = [None, None]
    te_dA = [None, None]
    hg_done = {}
    prev_ge = [None, None]
    gex_free = [None, None]
    for ec in range(4):
        wb = ec % 2
        wldA[wb] = rec.dma("gpsimd", wbuf[wb][:, 0:4096], a1_ap[ec],
                           waits=(te_dA[wb],), sem=s[f"sw{wb}"])
        mm = [None] * NGRP
        for ic in range(NIC):
            for grp in range(NGRP):
                waits = ((wldA[wb], hg_done.get((ec - 1, grp)))
                         if ic == 0 else ())
                xsl = bf[:, BA + ic * T + grp * GW: BA + ic * T + (grp + 1) * GW]
                mm[grp] = rec.op(
                    "tensor",
                    lambda e, wt=wbuf[wb][:, ic * 128:(ic + 1) * 128], xs=xsl,
                    f=(ic == 0), l=(ic == NIC - 1), g=grp:
                    e.matmul(out=ps[g][:, :], lhsT=wt, rhs=xs, start=f, stop=l),
                    inc=s["te"], waits=waits)
        te_dA[wb] = mm[NGRP - 1]
        for grp in range(NGRP):
            b = grp % 2
            gx = rec.op("tensor",
                        lambda e, g=grp, c=ec, b_=b: e.matmul(
                            out=ps[6 + b_][:, :],
                            lhsT=exp64[:, c * 128:(c + 1) * 128],
                            rhs=gT[:, g * GW:(g + 1) * GW],
                            start=True, stop=True),
                        inc=s["te"], waits=(mm[grp], exld, prev_ge[b]))
            ge = rec.op("scalar", lambda e, b_=b: e.activation(
                out=gexs[b_], in_=ps[6 + b_][:, :], func=AF.Copy),
                inc=s["act"], waits=(gx, gex_free[b]))
            prev_ge[b] = ge
            hgm = rec.op("vector",
                         lambda e, g=grp, c=ec, b_=b:
                         e.tensor_mul(hg[:, c * T + g * GW:c * T + (g + 1) * GW],
                                      ps[g][:, :], gexs[b_]),
                         inc=s["dve"], waits=(ge,))
            hg_done[(ec, grp)] = hgm
            gex_free[b] = hgm

    te_done = [te_dA[0], te_dA[1]]
    st_done = [None, None]
    ev_done = [None, None]
    ps_free = [None] * 6
    for g in range(NGRP):
        ps_free[g] = hg_done[(3, g)]
    xld = [None, None]
    wld = [None, None]
    for oc in range(NIC):
        wb = oc % 2
        wld[wb] = rec.dma(
            "gpsimd",
            wbuf[wb][:, 0:512].rearrange("p (ec o) -> p ec o", o=128),
            b1_ap.rearrange("(ec p) o -> p ec o", p=128)[:, :, oc * 128:(oc + 1) * 128],
            waits=(te_done[wb],), sem=s[f"sw{wb}"])
        mm = [None] * NGRP
        bk = [(oc * NGRP + grp) % 6 for grp in range(NGRP)]
        for ec in range(4):
            for grp in range(NGRP):
                waits = (wld[wb], ps_free[bk[grp]]) if ec == 0 else ()
                if oc == 0:
                    waits = waits + (hg_done[(ec, grp)],)
                hsl = hg[:, ec * T + grp * GW:ec * T + (grp + 1) * GW]
                mm[grp] = rec.op(
                    "tensor",
                    lambda e, wt=wbuf[wb][:, ec * 128:(ec + 1) * 128], hs=hsl,
                    f=(ec == 0), l=(ec == 3), g=bk[grp]:
                    e.matmul(out=ps[g][:, :], lhsT=wt, rhs=hs, start=f, stop=l),
                    inc=s["te"], waits=waits)
        te_done[wb] = mm[NGRP - 1]
        for grp in range(NGRP):
            b = grp % 2
            gsl = slice(grp * GW, (grp + 1) * GW)
            xld[b] = rec.dma("scalar", xold[b], x_in[oc * 128:(oc + 1) * 128, gsl],
                             waits=(ev_done[b],), sem=s[f"sx{b}"])
            ev = rec.op("vector",
                        lambda e, g=bk[grp], o=stgf[b], xo=xold[b]:
                        e.tensor_add(o, ps[g][:, :], xo),
                        inc=s["dve"], waits=(mm[grp], xld[b], st_done[b]))
            ev_done[b] = ev
            ps_free[bk[grp]] = ev
            st_done[b] = rec.dma("sync", x_out[oc * 128:(oc + 1) * 128, gsl],
                                 stgf[b], waits=(ev,), sem=s[f"ss{b}"])
    rec.barrier()



PHASES = {
    "qkv0": lambda rec, bufs, io, fu: (ph_ln(rec, bufs, io, "x0"),
                                       ph_qkv0(rec, bufs, io, fu)),
    "attn0": lambda rec, bufs, io, fu: (ph_attn(rec, bufs, io, 0, fu),
                                        ph_oproj(rec, bufs, io, 0)),
    "ffn0": lambda rec, bufs, io, fu: (ph_ln(rec, bufs, io, "x1"),
                                       ph_ffn0(rec, bufs, io)),
    "qkv1": lambda rec, bufs, io, fu: (ph_ln(rec, bufs, io, "x2"),
                                       ph_qkv1(rec, bufs, io, fu)),
    "attn1": lambda rec, bufs, io, fu: (ph_attn(rec, bufs, io, 1, fu),
                                        ph_oproj(rec, bufs, io, 1)),
    "ffn1": lambda rec, bufs, io, fu: (ph_ln(rec, bufs, io, "x3"),
                                       ph_ffn1(rec, bufs, io)),
}

_DRAM_SPECS = {
    "x0": ((D, T), F32), "x1": ((D, T), F32), "x2": ((D, T), F32),
    "x3": ((D, T), F32), "x4": ((D, T), F32),
    "qd0": ((D, T), BF16), "qd1": ((D, T), BF16),
    "kvl0": ((KV0_ROWS, T), BF16), "kvl1": ((KV1_ROWS, T), BF16),
    "kvg0": ((2 * KV0_ROWS, T), BF16), "kvg1": ((2 * KV1_ROWS, T), BF16),
    "qn0": ((1, 32), F32), "qn1": ((1, 32), F32),
}

_PH_READS = {
    "qkv0": ["x0"], "attn0": ["qd0", "kvg0", "qn0", "x0"], "ffn0": ["x1"],
    "qkv1": ["x2"], "attn1": ["qd1", "kvg1", "qn1", "x2"], "ffn1": ["x3"],
}
_PH_WRITES = {
    "qkv0": ["qd0", "kvl0", "qn0"], "attn0": ["x1"], "ffn0": ["x2"],
    "qkv1": ["qd1", "kvl1", "qn1"], "attn1": ["x3"], "ffn1": ["x4"],
}


def build(phase_names, fused):
    nc = bass.Bass("TRN2", target_bir_lowering=False)
    io = IO(nc)
    reads = set()
    writes = set()
    for p in phase_names:
        reads |= set(_PH_READS[p])
        writes |= set(_PH_WRITES[p])
    if fused:
        writes |= {"kvg0", "kvg1"}
    for name in sorted(reads | writes):
        shape, dt = _DRAM_SPECS[name]
        if name == "x4" and name in writes:
            kind = "ExternalOutput"
        elif name in writes and name in reads:
            kind = "Internal"
        elif name in writes:
            kind = "Internal" if fused else "ExternalOutput"
        else:
            kind = "ExternalInput"
        io.t(name, shape, dt, kind)

    with ExitStack() as st:
        semnames = ["qs", "qg", "qa", "te", "dve", "act", "cc0", "cc1",
                    "sw0", "sw1", "sx0", "sx1", "ss0", "ss1", "ss2", "ss3",
                    "skv0", "skv1", "sqt0", "sqt1", "sv0", "sv1", "sc0", "sc1"]
        sems = {nm: _Sem(st.enter_context(nc.semaphore(nm))) for nm in semnames}
        bfh = st.enter_context(nc.sbuf_tensor("bfa", [128, BF_COLS], BF16))
        f32h = st.enter_context(nc.sbuf_tensor("f32a", [128, F32_COLS], F32))
        psh = [st.enter_context(nc.psum_tensor(f"ps{i}", [128, 512], F32))
               for i in range(8)]
        rec = Rec(nc, sems)
        bufs = Bufs(bfh[:], f32h[:], [p[:] for p in psh])
        rec.btile = bufs.f32[0:1, FP + 460:FP + 461]
        rec.bscrap = bufs.f32[0:1, FP + 461:FP + 462]
        rec.psw = bufs.ps[7][0:1, 0:1]

        ph_prologue(rec, bufs, io)
        for p in phase_names:
            PHASES[p](rec, bufs, io, fused)
        rec.barrier()

        with nc.Block() as block:
            @block.sync
            def _(e):
                for f in rec.streams["sync"]:
                    f(e)

            @block.gpsimd
            def _(e):
                for f in rec.streams["gpsimd"]:
                    f(e)

            @block.tensor
            def _(e):
                for f in rec.streams["tensor"]:
                    f(e)

            @block.vector
            def _(e):
                for f in rec.streams["vector"]:
                    f(e)

            @block.scalar
            def _(e):
                for f in rec.streams["scalar"]:
                    f(e)

    return nc, io



_UNFUSED_PLAN = [["qkv0"], ["attn0"], ["ffn0"], ["qkv1"], ["attn1"], ["ffn1"]]
_FUSED_PLAN = [["qkv0", "attn0", "ffn0", "qkv1", "attn1", "ffn1"]]

PLAN = _FUSED_PLAN


def _run_plan(plan, sh, percore):
    pool = [dict(pc) for pc in percore]
    for c in range(NCORES):
        pool[c].update(sh)
    for launch in plan:
        fused = len(launch) == len(PHASES)
        nc, io = build(launch, fused)
        names = io.input_names
        in_maps = [{n: np.ascontiguousarray(pool[c][n]) for n in names}
                   for c in range(NCORES)]
        res = run_bass_kernel_spmd(nc, in_maps, core_ids=list(range(NCORES)))
        for c in range(NCORES):
            for n in io.output_names:
                pool[c][n] = res.results[c][n]
        for L, nm in ((0, "kvl0"), (1, "kvl1")):
            if nm in io.output_names:
                for c in range(NCORES):
                    p0 = c - c % 2
                    pool[c][f"kvg{L}"] = np.concatenate(
                        [pool[p0][nm], pool[p0 + 1][nm]], axis=0)
    return pool


def kernel(**inputs):
    sh, percore = _prep(inputs)
    pool = _run_plan(PLAN, sh, percore)
    out = np.empty((4, 4096, 4096), dtype=np.float32)
    for c in range(NCORES):
        s, r = c // 2, c % 2
        tiles = [r + 2 * i for i in range(NT)]
        tok = np.concatenate([np.arange(g * 128, (g + 1) * 128) for g in tiles])
        out[s][tok] = np.ascontiguousarray(pool[c]["x4"]).T
    return out


# version 24 (final)
# speedup vs baseline: 1.1424x; 1.1424x over previous
import math
from contextlib import ExitStack

import ml_dtypes
import numpy as np

import concourse.bass as bass
import concourse.mybir as mybir
from concourse.ap import AP as _AP
from concourse.bass_utils import run_bass_kernel_spmd

F32 = mybir.dt.float32
BF16 = mybir.dt.bfloat16
I32 = mybir.dt.int32
AF = mybir.ActivationFunctionType
ALU = mybir.AluOpType
AX = mybir.AxisListType

D = 4096
NCORES = 8
T = 2048
NT = 16
NGRP = 4
GW = 512
NIC = 32
HD = 128
NH = 32
NKV = 8
EPS = 1e-5
ISQ = 1.0 / math.sqrt(HD)

KV0_K, KV0_V, KV0_N, KV0_ROWS = 0, 4096, 8192, 8320
KV1_K, KV1_V, KV1_N, KV1_ROWS = 0, 1024, 2048, 2176

BF_COLS = 89088
F32_COLS = 8192
BA = 0
BB = 65536
BC = 82944
FP = 0
FS = 512

P_ONES = 0
P_ONES1P = 1
P_MAXQ2 = 136
P_BQ = 200
P_IDEN = 328

DBG_ATTN = False
DBG_FFN = False
PARANOID = set()



def _masks_for_parity(r):
    masks = np.zeros((8, 128, 512), dtype=np.float32)
    k = np.arange(128)[:, None]
    c = np.arange(512)[None, :]
    rhs = 2 * (c // 128) * 128 + (c % 128)
    for m in range(8):
        d = m - r
        masks[m] = 1.0 if d < 0 else (d * 128 + k <= rhs).astype(np.float32)
    return masks.astype(ml_dtypes.bfloat16)


def _rope_tables(positions):
    inv_freq = 1.0 / (10000.0 ** (np.arange(0, HD, 2, dtype=np.float64) / HD))
    freqs = positions[None, :].astype(np.float64) * inv_freq[:, None]
    cos = np.cos(freqs)
    sin = np.sin(freqs)
    cos_full = np.concatenate([cos, cos], axis=0).astype(np.float32)
    sin_swap = np.concatenate([sin, -sin], axis=0).astype(np.float32)
    return cos_full, sin_swap


def _tile_w(w):
    inn, out = w.shape
    return np.ascontiguousarray(
        w.reshape(inn // 128, 128, out // 128, 128)
        .transpose(2, 1, 0, 3).reshape(out // 128, 128, inn * 128 // 128))


def _prep(inputs):
    f32 = np.float32
    c = np.ascontiguousarray
    sh = {}
    sh["wqT"] = _tile_w(np.asarray(inputs["L0_attn_Wq"], f32).T)
    sh["wkT"] = _tile_w(np.asarray(inputs["L0_attn_Wk"], f32).T)
    sh["wvT"] = _tile_w(np.asarray(inputs["L0_attn_Wv"], f32).T)
    sh["woT"] = _tile_w(np.asarray(inputs["L0_attn_Wo"], f32).T)
    sh["b4"] = c(np.stack([
        np.asarray(inputs["L0_attn_bq"], f32).reshape(NIC, 128).T,
        np.asarray(inputs["L0_attn_bk"], f32).reshape(NIC, 128).T,
        np.asarray(inputs["L0_attn_bv"], f32).reshape(NIC, 128).T,
        np.asarray(inputs["L0_attn_bo"], f32).reshape(NIC, 128).T,
    ], axis=0))
    sh["basewT"] = _tile_w(np.asarray(inputs["L0_ffn_base_W"], f32).T)
    sh["gatewT"] = c(np.asarray(inputs["L0_ffn_gate_W"], f32).T)
    sh["a2T"] = c(np.asarray(inputs["L0_ffn_lora_A"], f32).reshape(224, D).T)
    sh["b2"] = c(np.asarray(inputs["L0_ffn_lora_B"], f32).transpose(0, 2, 1).reshape(224, D))
    sh["qwq"] = _tile_w(np.asarray(inputs["L1_attn_qw_q"], np.int32))
    sh["qwk"] = _tile_w(np.asarray(inputs["L1_attn_qw_k"], np.int32))
    sh["qwv"] = _tile_w(np.asarray(inputs["L1_attn_qw_v"], np.int32))
    sh["qwo"] = _tile_w(np.asarray(inputs["L1_attn_qw_o"], np.int32))

    def _tile_sc(scm):
        g, out = scm.shape
        return np.ascontiguousarray(
            scm.reshape(g, out // 128, 128).transpose(1, 0, 2).reshape(out // 128, g * 128))

    sh["scq"] = _tile_sc(np.asarray(inputs["L1_attn_scales_q"], f32))
    sh["sck"] = _tile_sc(np.asarray(inputs["L1_attn_scales_k"], f32))
    sh["scv"] = _tile_sc(np.asarray(inputs["L1_attn_scales_v"], f32))
    sh["sco"] = _tile_sc(np.asarray(inputs["L1_attn_scales_o"], f32))
    sh["a1T"] = _tile_w(np.asarray(inputs["L1_ffn_A_w"], f32).T)
    sh["b1T"] = c(np.asarray(inputs["L1_ffn_B_w"], f32).T)
    sh["wqwT"] = c(np.asarray(inputs["L1_ffn_Wq_w"], f32).T)
    sh["keysT"] = c(np.asarray(inputs["L1_ffn_keys"], f32).T)

    sh["iden"] = np.eye(128, dtype=f32)
    ex28 = np.zeros((28, 224), f32)
    for e in range(28):
        ex28[e, e * 8:(e + 1) * 8] = 2.0
    sh["ex28"] = ex28
    ex64 = np.zeros((64, 512), f32)
    for e in range(64):
        ex64[e, e * 8:(e + 1) * 8] = 2.0
    sh["ex64"] = ex64

    x = np.asarray(inputs["L0_attn_x"], f32)
    percore = []
    for core in range(NCORES):
        s, r = core // 2, core % 2
        tiles = [r + 2 * i for i in range(NT)]
        tok = np.concatenate([np.arange(g * 128, (g + 1) * 128) for g in tiles])
        d = {"x0": c(x[s][tok].T)}
        cos, sin = _rope_tables(tok)
        d["ropec"] = cos
        d["ropes"] = sin
        d["masks"] = _masks_for_parity(r)
        percore.append(d)
    return sh, percore



class _Sem:
    __slots__ = ("h", "cnt")

    def __init__(self, h):
        self.h = h
        self.cnt = 0


class Rec:
    ENGINES = ("sync", "gpsimd", "tensor", "vector", "scalar")

    def __init__(self, nc, sems):
        self.nc = nc
        self.streams = {e: [] for e in self.ENGINES}
        self.s = sems
        self.qsem = {"sync": sems["qs"], "gpsimd": sems["qg"], "scalar": sems["qa"]}

    def wait(self, eng, *deps):
        for dep in deps:
            if dep is None:
                continue
            q, v = dep
            if v <= 0:
                continue
            self.streams[eng].append(lambda e, h=q.h, v=v: e.wait_ge(h, v))

    _ESEM = {"tensor": "te", "vector": "dve", "scalar": "act"}
    paranoid = False

    SYNC_STREAMS = ("qs", "sx0", "sx1", "ss0", "ss1", "ss2", "ss3",
                    "skv0", "skv1", "sqt0", "sqt1", "sv0", "sv1", "sc0", "sc1")
    GP_STREAMS = ("qg", "sw0", "sw1")

    def _paranoid_wait(self, eng):
        for nm in self.SYNC_STREAMS + self.GP_STREAMS + ("qa", "te", "dve", "act"):
            sem = self.s[nm]
            if sem.cnt > 0:
                self.streams[eng].append(
                    lambda e, h=sem.h, v=sem.cnt: e.wait_ge(h, v))

    def op(self, eng, fn, inc=None, n=1, waits=(), fence=False):
        if self.paranoid:
            self._paranoid_wait(eng)
            if inc is None:
                inc = self.s[self._ESEM[eng]]
            fence = True
        self.wait(eng, *waits)
        if fence and inc is None:
            inc = self.s[self._ESEM[eng]]
        if inc is None:
            self.streams[eng].append(fn)
            return None
        self.streams[eng].append(lambda e, fn=fn, h=inc.h, n=n: fn(e).then_inc(h, n))
        inc.cnt += n
        mark = (inc, inc.cnt)
        if fence:
            self.wait(eng, mark)
        return mark

    def dma(self, eng, out, in_, waits=(), tr=False, sem=None):
        q = sem if sem is not None else self.qsem[eng]
        if self.paranoid:
            self._paranoid_wait(eng)
        self.wait(eng, *waits)
        if tr:
            self.streams[eng].append(
                lambda e, o=out, i=in_, h=q.h: e.dma_start_transpose(o, i).then_inc(h, 16))
        else:
            self.streams[eng].append(
                lambda e, o=out, i=in_, h=q.h: e.dma_start(out=o, in_=i).then_inc(h, 16))
        q.cnt += 16
        return (q, q.cnt)

    def barrier(self):
        marks = {}
        bt = self.btile
        bs = self.bscrap
        pw = self.psw
        for eng, nms in (("sync", self.SYNC_STREAMS), ("gpsimd", self.GP_STREAMS)):
            for nm in nms:
                sem = self.s[nm]
                if sem.cnt > 0:
                    self.streams[eng].append(
                        lambda e, h=sem.h, v=sem.cnt: e.wait_ge(h, v))
            sem = self.s[nms[0]]
            self.streams[eng].append(lambda e, h=sem.h: e.sem_inc(h, 16))
            sem.cnt += 16
            marks[eng] = (sem, sem.cnt)
        dve = self.s["dve"]
        self.streams["vector"].append(
            lambda e, h=dve.h: e.memset(bt, 1.0).then_inc(h, 1))
        dve.cnt += 1
        marks["vector"] = (dve, dve.cnt)
        te = self.s["te"]
        self.wait("tensor", marks["vector"])
        self.streams["tensor"].append(
            lambda e, h=te.h: e.matmul(out=pw, lhsT=bt, rhs=bt,
                                       start=True, stop=True).then_inc(h, 1))
        te.cnt += 1
        marks["tensor"] = (te, te.cnt)
        act = self.s["act"]
        self.wait("scalar", marks["vector"])
        self.streams["scalar"].append(
            lambda e, h=act.h: e.activation(out=bs, in_=bt,
                                            func=AF.Copy).then_inc(h, 1))
        act.cnt += 1
        marks["scalar"] = (act, act.cnt)
        for eng in self.ENGINES:
            self.wait(eng, *[m for k, m in marks.items() if k != eng])


class IO:
    def __init__(self, nc):
        self.nc = nc
        self.tensors = {}
        self.kinds = {}

    def t(self, name, shape=None, dtype=None, kind="Internal"):
        if name in self.tensors:
            return self.tensors[name]
        h = self.nc.dram_tensor(name, list(shape), dtype, kind=kind)
        self.tensors[name] = h
        self.kinds[name] = kind
        return h

    def ap(self, name, shape=None, dtype=None, kind="Internal"):
        return self.t(name, shape, dtype, kind).ap()

    @property
    def input_names(self):
        return [n for n, k in self.kinds.items() if k == "ExternalInput"]

    @property
    def output_names(self):
        return [n for n, k in self.kinds.items() if k == "ExternalOutput"]


class Bufs:
    def __init__(self, bf, f32, ps):
        self.bf = bf
        self.f32 = f32
        self.ps = ps


def _wtile_src(w_ap, oc, width=128):
    return w_ap.rearrange("(ic p) o -> p ic o", p=128)[:, :, oc * width:(oc + 1) * width]



def ph_prologue(rec, bufs, io):
    f32 = bufs.f32
    b4 = io.ap("b4", (4, 128, 32), F32, "ExternalInput")
    iden = io.ap("iden", (128, 128), F32, "ExternalInput")
    rec.dma("sync", f32[:, FP + P_BQ:FP + P_BQ + 128], b4.rearrange("b p c -> p b c"))
    rec.dma("sync", f32[:, FP + P_IDEN:FP + P_IDEN + 128], iden)
    rec.op("vector", lambda e: e.memset(f32[:, FP + P_ONES:FP + P_ONES + 1], 1.0))
    rec.op("vector", lambda e: e.memset(f32[0:1, FP + P_ONES1P:FP + P_ONES1P + 128], 1.0))
    rec.barrier()


def ph_ln(rec, bufs, io, x_name):
    rec.paranoid = "ln" in PARANOID
    s = rec.s
    bf, f32, ps = bufs.bf, bufs.f32, bufs.ps
    x = io.tensors[x_name].ap()
    xt = [f32[:, FS + k * 512:FS + (k + 1) * 512] for k in range(3)]
    sq2 = [bf[:, BC:BC + 512], bf[:, BC + 512:BC + 1024]]
    r_bc = [bf[:, BC + 1024 + k * 512:BC + 1536 + k * 512] for k in range(2)]
    mr_bc = [bf[:, BC + 2048 + k * 512:BC + 2560 + k * 512] for k in range(2)]
    ones_b = bf[:, BC + 3072:BC + 3073]
    m_row = [f32[0:1, FS + 1536 + k * 512:FS + 2048 + k * 512] for k in range(2)]
    v_row = [f32[0:1, FS + 2560 + k * 512:FS + 3072 + k * 512] for k in range(2)]
    m2_row = [f32[0:1, FS + 3584 + k * 512:FS + 4096 + k * 512] for k in range(2)]
    r_row = [f32[0:1, FS + 4608 + k * 512:FS + 5120 + k * 512] for k in range(2)]
    mr_row = [f32[0:1, FS + 5632 + k * 512:FS + 6144 + k * 512] for k in range(2)]
    ones1p = f32[0:1, FP + P_ONES1P:FP + P_ONES1P + 128]

    ones_rdy = rec.op("vector", lambda e: e.memset(ones_b, 1.0), inc=s["dve"])

    _Q = ("sync", "gpsimd", "scalar")
    _QS = ("sx0", "sx1", "sc0")
    ld = [None, None, None]
    cvt_done = [None, None, None]
    sqmm_done = [None, None]
    stats_mm = [None] * NGRP
    grp_cvt = [None] * NGRP
    rows_read = [None] * NGRP
    c_mark = [None, None]
    apply_last = [None, None]
    _PR = lambda g: (g % 3) * 32

    def stats(grp):
        gsl = slice(grp * GW, (grp + 1) * GW)
        mm_sum = mm_sq = None
        for ic in range(NIC):
            b = ic % 3
            b2 = ic % 2
            ld[b] = rec.dma(_Q[b], xt[b], x[ic * 128:(ic + 1) * 128, gsl],
                            waits=(cvt_done[b],), sem=s[_QS[b]])
            xsl = bf[:, BA + ic * T + grp * GW: BA + ic * T + (grp + 1) * GW]
            cvt = rec.op("scalar", lambda e, i=xt[b], o=xsl: e.activation(
                out=o, in_=i, func=AF.Copy), inc=s["act"],
                waits=(ld[b],))
            cvt_done[b] = cvt
            sqa = rec.op("scalar", lambda e, i=xsl, o=sq2[b2]: e.activation(
                out=o, in_=i, func=AF.Square), inc=s["act"],
                waits=(sqmm_done[b2],))
            w0 = (cvt,)
            if ic == 0:
                w0 += (ones_rdy,)
                if grp == 3:
                    w0 += (rows_read[0],)
            mm_sum = rec.op("tensor",
                            lambda e, i=xsl, p=_PR(grp), f=(ic == 0), l=(ic == NIC - 1):
                            e.matmul(out=ps[6][p:p + 1, :], lhsT=ones_b, rhs=i,
                                     start=f, stop=l),
                            inc=s["te"], waits=w0)
            mm_sq = rec.op("tensor",
                           lambda e, o=sq2[b2], p=_PR(grp), f=(ic == 0), l=(ic == NIC - 1):
                           e.matmul(out=ps[7][p:p + 1, :], lhsT=ones_b, rhs=o,
                                    start=f, stop=l),
                           inc=s["te"], waits=(sqa,))
            sqmm_done[b2] = mm_sq
        stats_mm[grp] = (mm_sum, mm_sq)
        grp_cvt[grp] = tuple(cvt_done)

    def rows_bcast(grp):
        pp = grp % 2
        mr, vr, m2, rr, mrr = (m_row[pp], v_row[pp], m2_row[pp],
                               r_row[pp], mr_row[pp])
        rec.op("vector", lambda e, p=_PR(grp): e.tensor_scalar(
            out=mr, in0=ps[6][p:p + 1, :], scalar1=1.0 / D, scalar2=None,
            op0=ALU.mult), waits=stats_mm[grp], fence=True)
        rows_read[grp] = rec.op("vector", lambda e, p=_PR(grp): e.tensor_scalar(
            out=vr, in0=ps[7][p:p + 1, :], scalar1=1.0 / D, scalar2=EPS,
            op0=ALU.mult, op1=ALU.add), inc=s["dve"], fence=True)
        rec.op("vector", lambda e: e.tensor_mul(m2, mr, mr), fence=True)
        dv = rec.op("vector", lambda e: e.tensor_sub(vr, vr, m2),
                    inc=s["dve"], fence=True)
        la = rec.op("scalar", lambda e: e.activation(
            out=rr, in_=vr, func=AF.Ln), inc=s["act"], waits=(dv,), fence=True)
        ea = rec.op("scalar", lambda e: e.activation(
            out=rr, in_=rr, func=AF.Exp, scale=-0.5), inc=s["act"])
        dmr = rec.op("vector", lambda e: e.tensor_mul(mrr, mr, rr),
                     inc=s["dve"], waits=(ea,))
        t1 = rec.op("tensor", lambda e: e.matmul(
            out=ps[4][:, :], lhsT=ones1p, rhs=rr, start=True, stop=True),
            inc=s["te"], waits=(dmr, c_mark[pp]))
        t2 = rec.op("tensor", lambda e: e.matmul(
            out=ps[5][:, :], lhsT=ones1p, rhs=mrr, start=True, stop=True),
            inc=s["te"])
        rec.op("vector", lambda e, o=r_bc[pp]: e.tensor_copy(out=o, in_=ps[4][:, :]),
               inc=s["dve"], waits=(t1, t2, apply_last[pp]))
        c_mark[pp] = rec.op("vector", lambda e, o=mr_bc[pp]: e.tensor_copy(
            out=o, in_=ps[5][:, :]), inc=s["dve"])

    def apply(grp):
        pp = grp % 2
        last = None
        for ic in range(NIC):
            xsl = bf[:, BA + ic * T + grp * GW: BA + ic * T + (grp + 1) * GW]
            rec.op("vector", lambda e, i=xsl: e.tensor_mul(i, i, r_bc[pp]),
                   waits=(c_mark[pp],) + grp_cvt[grp] if ic == 0 else ())
            last = rec.op("vector", lambda e, o=xsl: e.tensor_sub(o, o, mr_bc[pp]),
                          inc=s["dve"])
        apply_last[pp] = last

    stats(0)
    stats(1)
    rows_bcast(0)
    apply(0)
    stats(2)
    rows_bcast(1)
    apply(1)
    stats(3)
    rows_bcast(2)
    apply(2)
    rows_bcast(3)
    apply(3)
    rec.barrier()


class _NormState:

    def __init__(self, rec, bufs):
        self.rec = rec
        self.bufs = bufs
        self.i = 0
        self.upd = [None] * 4
        self.sq = [bufs.bf[:, BC + 2112 + k * 512:BC + 2624 + k * 512]
                   for k in range(4)]
        self.ones_b = bufs.bf[:, BC + 4160:BC + 4161]
        self.t11 = [bufs.f32[0:1, FS + 7170 + k:FS + 7171 + k] for k in range(4)]
        self.prow = [bufs.ps[6][0:1, :], bufs.ps[7][0:1, :],
                     bufs.ps[6][32:33, :], bufs.ps[7][32:33, :]]
        self.pending = []
        self.ones_rdy = rec.op("vector", lambda e: e.memset(self.ones_b, 1.0),
                               inc=rec.s["dve"])

    def update(self, stage_ap, tgt):
        rec = self.rec
        s = rec.s
        k = self.i % 4
        self.i += 1
        sqa = rec.op("scalar", lambda e, o=self.sq[k]: e.activation(
            out=o, in_=stage_ap, func=AF.Square), inc=s["act"],
            waits=(self.upd[k],))
        self.pending.append((k, tgt, sqa))
        return sqa, sqa

    def flush(self):
        rec, bufs = self.rec, self.bufs
        s = rec.s
        last = None
        for k, tgt, sqa in self.pending:
            nm = rec.op("tensor", lambda e, o=self.sq[k], pk=self.prow[k]:
                        e.matmul(out=pk, lhsT=self.ones_b, rhs=o,
                                 start=True, stop=True),
                        inc=s["te"], waits=(sqa, self.ones_rdy))
            self.ones_rdy = None
            rec.op("vector", lambda e, t=self.t11[k], pk=self.prow[k]:
                   e.tensor_reduce(out=t, in_=pk, axis=AX.X, op=ALU.max),
                   waits=(nm,), fence=True)
            upd = rec.op("vector", lambda e, t=self.t11[k], tg=tgt:
                         e.tensor_max(tg, tg, t), inc=s["dve"])
            self.upd[k] = upd
            last = upd
        self.pending = []
        return last


def _emit_norms_and_gather(rec, bufs, io, L, maxk2, norm_prev, fused):
    s = rec.s
    bf = bufs.bf
    kvl = io.tensors[f"kvl{L}"].ap()
    nrow = KV0_N if L == 0 else KV1_N
    nrow_bf = bf[0:1, BC + 2048:BC + 2112]
    dv = rec.op("vector", lambda e: e.tensor_scalar(
        out=nrow_bf[0:1, 0:32], in0=maxk2, scalar1=1.0404, scalar2=None,
        op0=ALU.mult), inc=s["dve"], waits=(norm_prev,), fence=True)
    nst = rec.dma("sync", kvl[nrow:nrow + 1, 0:64], nrow_bf, waits=(dv,))
    if fused:
        kvg = io.tensors[f"kvg{L}"].ap()
        cc = s[f"cc{L}"]
        waits = [nst] + [(s[f"ss{g}"], s[f"ss{g}"].cnt) for g in range(NGRP)]
        rec.op("gpsimd", lambda e: e.collective_compute(
            "AllGather", ALU.bypass,
            replica_groups=[[0, 1], [2, 3], [4, 5], [6, 7]],
            ins=[kvl], outs=[kvg]), inc=cc, waits=tuple(waits))


def _emit_qn(rec, bufs, io, L, maxq2, norm_prev):
    s = rec.s
    f32 = bufs.f32
    qn = io.tensors[f"qn{L}"].ap()
    qrow = f32[0:1, FS + 480:FS + 512]
    dv = rec.op("vector", lambda e: e.tensor_scalar(
        out=qrow, in0=maxq2, scalar1=1.01, scalar2=None,
        op0=ALU.mult), inc=s["dve"], waits=(norm_prev,))
    rec.dma("sync", qn, qrow, waits=(dv,))


def ph_qkv0(rec, bufs, io, fused=False):
    rec.paranoid = "qkv0" in PARANOID
    s = rec.s
    bf, f32, ps = bufs.bf, bufs.f32, bufs.ps
    wq = io.ap("wqT", (NIC, 128, D), F32, "ExternalInput")
    wk = io.ap("wkT", (NIC, 128, D), F32, "ExternalInput")
    wv = io.ap("wvT", (NIC, 128, D), F32, "ExternalInput")
    qd = io.tensors["qd0"].ap()
    kvl = io.tensors["kvl0"].ap()

    wbuf = [bf[:, BB:BB + 4096], bf[:, BB + 4608:BB + 4608 + 4096]]
    stg = [bf[:, BC + i * 512:BC + (i + 1) * 512] for i in range(4)]
    norms = _NormState(rec, bufs)
    maxq2 = f32[0:1, FP + P_MAXQ2:FP + P_MAXQ2 + 32]
    maxk2 = f32[0:1, FS + 520:FS + 552]
    rec.op("vector", lambda e: e.memset(maxk2, 0.0))
    rec.op("vector", lambda e: e.memset(maxq2, 0.0))

    specs = [(wk, 1, kvl, KV0_K, "k"), (wv, 2, kvl, KV0_V, "v"),
             (wq, 0, qd, 0, "q")]
    wld = [None, None]
    te_done = [None, None]
    st_done = [None] * NGRP
    ps_free = [None] * 6
    sq_guard = [None] * NGRP
    norm_prev = None
    spec_chain = 0
    for w, bidx, dst, dbase, kind in specs:
        if kind == "q":
            _emit_norms_and_gather(rec, bufs, io, 0, maxk2, norm_prev, fused)
        for oc in range(NIC):
            wb = oc % 2
            wld[wb] = rec.dma("gpsimd", wbuf[wb], w[oc],
                              waits=(te_done[wb],), sem=s[f"sw{wb}"])
            mm = [None] * NGRP
            bk = [(spec_chain + grp) % 6 for grp in range(NGRP)]
            for ic in range(NIC):
                for grp in range(NGRP):
                    waits = (wld[wb], ps_free[bk[grp]]) if ic == 0 else ()
                    xsl = bf[:, BA + ic * T + grp * GW: BA + ic * T + (grp + 1) * GW]
                    mm[grp] = rec.op(
                        "tensor",
                        lambda e, wt=wbuf[wb][:, ic * 128:(ic + 1) * 128], xs=xsl,
                        f=(ic == 0), l=(ic == NIC - 1), g=bk[grp]:
                        e.matmul(out=ps[g][:, :], lhsT=wt, rhs=xs, start=f, stop=l),
                        inc=s["te"], waits=waits)
                if ic == 1:
                    norm_prev = norms.flush() or norm_prev
            spec_chain += NGRP
            te_done[wb] = mm[NGRP - 1]
            bias_col = f32[:, FP + P_BQ + bidx * 32 + oc:FP + P_BQ + bidx * 32 + oc + 1]
            for grp in range(NGRP):
                gsl = slice(grp * GW, (grp + 1) * GW)
                ev = rec.op("vector",
                            lambda e, g=bk[grp], o=stg[grp], bc=bias_col:
                            e.tensor_scalar(out=o, in0=ps[g][:, :], scalar1=bc,
                                            scalar2=None, op0=ALU.add),
                            inc=s["dve"], waits=(mm[grp], st_done[grp], sq_guard[grp]))
                ps_free[bk[grp]] = ev
                if kind in ("q", "k"):
                    tgt = maxq2[0:1, oc:oc + 1] if kind == "q" else maxk2[0:1, oc:oc + 1]
                    rec.wait("scalar", ev)
                    _, sq_guard[grp] = norms.update(stg[grp], tgt)
                st_done[grp] = rec.dma(
                    "sync", dst[dbase + oc * 128:dbase + (oc + 1) * 128, gsl],
                    stg[grp], waits=(ev,), sem=s[f"ss{grp}"])
    norm_prev = norms.flush() or norm_prev
    _emit_qn(rec, bufs, io, 0, maxq2, norm_prev)
    rec.barrier()


def ph_qkv1(rec, bufs, io, fused=False):
    rec.paranoid = "qkv1" in PARANOID
    s = rec.s
    bf, f32, ps = bufs.bf, bufs.f32, bufs.ps
    qw = {"qwq": io.ap("qwq", (32, 128, D), I32, "ExternalInput"),
          "qwk": io.ap("qwk", (8, 128, D), I32, "ExternalInput"),
          "qwv": io.ap("qwv", (8, 128, D), I32, "ExternalInput")}
    sc = {"scq": io.t("scq", (32, 4096), F32, "ExternalInput"),
          "sck": io.t("sck", (8, 4096), F32, "ExternalInput"),
          "scv": io.t("scv", (8, 4096), F32, "ExternalInput")}
    qd = io.tensors["qd1"].ap()
    kvl = io.tensors["kvl1"].ap()
    ropec = io.ap("ropec", (128, T), F32, "ExternalInput")
    ropes = io.ap("ropes", (128, T), F32, "ExternalInput")

    wbuf = [bf[:, BB:BB + 4096], bf[:, BB + 4608:BB + 4608 + 4096]]
    scbc = [bf[:, BB + 9216:BB + 9216 + 4096], bf[:, BB + 13312:BB + 13312 + 4096]]
    stg = [bf[:, BC + i * 512:BC + (i + 1) * 512] for i in range(4)]
    cos = f32[:, FS:FS + 2048]
    sin = f32[:, FS + 2048:FS + 4096]
    rb = f32[:, FS + 5120:FS + 5632]
    sw = f32[:, FS + 5632:FS + 6144]
    t1b = f32[:, FS + 4096:FS + 4608]
    norms = _NormState(rec, bufs)
    maxq2 = f32[0:1, FP + P_MAXQ2:FP + P_MAXQ2 + 32]
    maxk2 = f32[0:1, FS + 7180:FS + 7212]

    ldc = rec.dma("sync", cos, ropec)
    lds = rec.dma("sync", sin, ropes)
    rec.op("vector", lambda e: e.memset(maxk2, 0.0), waits=(ldc, lds))
    rec.op("vector", lambda e: e.memset(maxq2, 0.0))

    specs = [("qwk", "sck", 8, kvl, KV1_K, True, "k"),
             ("qwv", "scv", 8, kvl, KV1_V, False, "v"),
             ("qwq", "scq", 32, qd, 0, True, "q")]
    wld = [None, None]
    te_done = [None, None]
    st_done = [None] * NGRP
    ps_free = [None] * 6
    sq_guard = [None] * NGRP
    dq_done = [None, None]
    norm_prev = None
    rope_chain = None
    spec_chain = 0
    pend_evac = None

    def emit_evac(pe):
        nonlocal rope_chain, norm_prev
        oc, mm, bkl, dst_, dbase_, rope_, kind_ = pe
        for grp in range(NGRP):
            gsl = slice(grp * GW, (grp + 1) * GW)
            if rope_:
                ev0 = rec.op("scalar",
                             lambda e, g=bkl[grp]: e.activation(
                                 out=rb, in_=ps[g][:, :], func=AF.Copy),
                             inc=s["act"],
                             waits=(mm[grp], rope_chain))
                ps_free[bkl[grp]] = ev0
                cs = cos[:, grp * GW:(grp + 1) * GW]
                ss = sin[:, grp * GW:(grp + 1) * GW]
                rec.op("vector", lambda e, c_=cs: e.tensor_mul(t1b, rb, c_),
                       waits=(ev0,))
                rec.op("vector", lambda e, s_=ss: e.tensor_mul(
                    sw[0:64, :], rb[64:128, :], s_[64:128, :]))
                rec.op("vector", lambda e, s_=ss: e.tensor_mul(
                    sw[64:128, :], rb[0:64, :], s_[0:64, :]))
                evm = rec.op("vector",
                             lambda e, o=stg[grp]: e.tensor_add(o, t1b, sw),
                             inc=s["dve"], waits=(st_done[grp], sq_guard[grp]))
                rope_chain = evm
            else:
                evm = rec.op("scalar",
                             lambda e, g=bkl[grp], o=stg[grp]:
                             e.activation(out=o, in_=ps[g][:, :], func=AF.Copy),
                             inc=s["act"],
                             waits=(mm[grp], st_done[grp], sq_guard[grp]))
                ps_free[bkl[grp]] = evm
            if kind_ in ("q", "k"):
                tgt = maxq2[0:1, oc:oc + 1] if kind_ == "q" else maxk2[0:1, oc:oc + 1]
                rec.wait("scalar", evm)
                _, sq_guard[grp] = norms.update(stg[grp], tgt)
            st_done[grp] = rec.dma(
                "sync", dst_[dbase_ + oc * 128:dbase_ + (oc + 1) * 128, gsl],
                stg[grp], waits=(evm,), sem=s[f"ss{grp}"])

    for wn, scn, nout, dst, dbase, rope, kind in specs:
        if kind == "q":
            if pend_evac is not None:
                emit_evac(pend_evac)
                pend_evac = None
            norm_prev = norms.flush() or norm_prev
            _emit_norms_and_gather(rec, bufs, io, 1, maxk2, norm_prev, fused)
        w_ap = qw[wn]
        for oc in range(nout):
            wb = oc % 2
            wld[wb] = rec.dma("gpsimd", wbuf[wb], w_ap[oc],
                              waits=(te_done[wb],), sem=s[f"sw{wb}"])
            h0 = rec.dma("gpsimd", scbc[wb][0:16, :],
                         _AP(sc[scn].ap().tensor, oc * 4096,
                             [[0, 16], [1, 4096]]),
                         waits=(dq_done[wb],), sem=s[f"sc{wb}"])
            h1 = rec.dma("scalar", scbc[wb][16:32, :], scbc[wb][0:16, :],
                         waits=(h0,), sem=s[f"sc{wb}"])
            h2 = rec.dma("scalar", scbc[wb][32:64, :], scbc[wb][0:32, :],
                         waits=(h1,), sem=s[f"sc{wb}"])
            scld = rec.dma("scalar", scbc[wb][64:128, :], scbc[wb][0:64, :],
                           waits=(h2,), sem=s[f"sc{wb}"])
            dq_done[wb] = rec.op(
                "vector", lambda e, wv=wbuf[wb], sv=scbc[wb]:
                e.scalar_tensor_tensor(out=wv, in0=wv, scalar=-8.0, in1=sv,
                                       op0=ALU.add, op1=ALU.mult),
                inc=s["dve"], waits=(wld[wb], scld, te_done[wb]))
            if pend_evac is not None:
                emit_evac(pend_evac)
            mm = [None] * NGRP
            bk = [(spec_chain + grp) % 6 for grp in range(NGRP)]
            for ic in range(NIC):
                for grp in range(NGRP):
                    waits = (dq_done[wb], ps_free[bk[grp]]) if ic == 0 else ()
                    xsl = bf[:, BA + ic * T + grp * GW: BA + ic * T + (grp + 1) * GW]
                    mm[grp] = rec.op(
                        "tensor",
                        lambda e, wt=wbuf[wb][:, ic * 128:(ic + 1) * 128], xs=xsl,
                        f=(ic == 0), l=(ic == NIC - 1), g=bk[grp]:
                        e.matmul(out=ps[g][:, :], lhsT=wt, rhs=xs, start=f, stop=l),
                        inc=s["te"], waits=waits)
                if ic == 1:
                    norm_prev = norms.flush() or norm_prev
            spec_chain += NGRP
            te_done[wb] = mm[NGRP - 1]
            pend_evac = (oc, mm, bk, dst, dbase, rope, kind)
    emit_evac(pend_evac)
    norm_prev = norms.flush() or norm_prev
    _emit_qn(rec, bufs, io, 1, maxq2, norm_prev)
    rec.barrier()


def ph_gather(rec, bufs, io, L):
    s = rec.s
    rows = KV0_ROWS if L == 0 else KV1_ROWS
    kvl = io.tensors[f"kvl{L}"].ap()
    kvg = io.tensors[f"kvg{L}"].ap()
    cc = s[f"cc{L}"]
    rec.op("gpsimd", lambda e: e.collective_compute(
        "AllGather", ALU.bypass, replica_groups=[[0, 1], [2, 3], [4, 5], [6, 7]],
        ins=[kvl], outs=[kvg]), inc=cc)
    for eng in Rec.ENGINES:
        rec.wait(eng, (cc, cc.cnt))
    rec.barrier()


def ph_attn(rec, bufs, io, L, fused=False):
    rec.paranoid = "attn" in PARANOID
    s = rec.s
    bf, f32, ps = bufs.bf, bufs.f32, bufs.ps
    rows = KV0_ROWS if L == 0 else KV1_ROWS
    kbase = KV0_K if L == 0 else KV1_K
    vbase = KV0_V if L == 0 else KV1_V
    nrow = KV0_N if L == 0 else KV1_N
    nkv = NH if L == 0 else NKV
    qpk = 1 if L == 0 else 4
    qd = io.tensors[f"qd{L}"].ap()
    kvg = io.tensors[f"kvg{L}"].ap()
    qn = io.tensors[f"qn{L}"].ap()
    masks_in = io.ap("masks", (8, 128, 512), BF16, "ExternalInput")

    ksb = [bf[:, BB:BB + 4096], bf[:, BB + 4096:BB + 8192]]
    vsb = [bf[:, BB + 8192:BB + 12288], bf[:, BB + 12288:BB + 16384]]
    masks = bf[:, BC:BC + 4096]
    qt = [bf[:, BC + 4096 + i * 512:BC + 4096 + (i + 1) * 512] for i in range(2)]
    pt = [bf[:, BC + 5120 + i * 512:BC + 5120 + (i + 1) * 512] for i in range(2)] + \
         [bf[:, BB + 16384 + i * 512:BB + 16896 + i * 512] for i in range(2)]
    ones_bf = masks[:, 511:512]
    k2a = f32[0:1, FS:FS + 32]
    k2b = f32[0:1, FS + 32:FS + 64]
    q2a = f32[0:1, FS + 64:FS + 96]
    lnq = f32[0:1, FS + 128:FS + 160]
    brow = f32[0:1, FS + 160:FS + 192]
    rrow = f32[0:1, FS + 192:FS + 704]
    rbc = f32[:, FS + 704:FS + 1216]
    biasall = f32[:, FS + 1216:FS + 1248]
    ones1p = f32[0:1, FP + P_ONES1P:FP + P_ONES1P + 128]

    ccm = (s[f"cc{L}"], 1) if fused else None
    mld = rec.dma("sync", masks, masks_in.rearrange("m p c -> p m c"))
    onesb_rdy = mld
    rec.wait("gpsimd", ccm)
    rec.dma("gpsimd", k2a, kvg[nrow:nrow + 1, 0:32])
    m2 = rec.dma("gpsimd", k2b, kvg[rows + nrow:rows + nrow + 1, 0:32])
    m3 = rec.dma("gpsimd", q2a, qn)
    d0 = rec.op("vector", lambda e: e.tensor_max(k2a, k2a, k2b), inc=s["dve"],
                waits=(m2, m3))
    l1 = rec.op("scalar", lambda e: e.activation(out=lnq, in_=q2a, func=AF.Ln),
                inc=s["act"], waits=(d0,))
    l2 = rec.op("scalar", lambda e: e.activation(out=k2b, in_=k2a, func=AF.Ln),
                inc=s["act"])
    if qpk == 1:
        ksrc = k2b
    else:
        ksrc = _AP(k2b.tensor, k2b.offset, [k2b.ap[0], [1, 8], [0, 4]])
    d1 = rec.op("vector", lambda e, ks=ksrc: e.scalar_tensor_tensor(
        out=brow, in0=lnq, scalar=-math.log(HD), in1=ks,
        op0=ALU.add, op1=ALU.add), inc=s["dve"], waits=(l1, l2))
    e1 = rec.op("scalar", lambda e: e.activation(
        out=brow, in_=brow, func=AF.Exp, scale=0.5),
        inc=s["act"], waits=(d1,))
    negb = rec.op("vector", lambda e: e.tensor_scalar(
        out=brow, in0=brow, scalar1=-1.0, scalar2=None, op0=ALU.mult),
        inc=s["dve"], waits=(e1,))
    bm = rec.op("tensor", lambda e: e.matmul(
        out=ps[5][:, 0:32], lhsT=ones1p, rhs=brow[0:1, 0:32],
        start=True, stop=True), inc=s["te"], waits=(negb,))
    bias_rdy = rec.op("vector", lambda e: e.tensor_copy(
        out=biasall, in_=ps[5][:, 0:32]), inc=s["dve"], waits=(bm,))
    if DBG_ATTN:
        dbg_b = io.ap(f"dbg_brow{L}", (1, 192), F32, "ExternalOutput")
        rec.dma("sync", dbg_b[0:1, 0:32], brow, waits=(negb,))

    kld = [None, None]
    vld = [None, None]
    te_free = [None, None]

    def load_kv(kvh, kb):
        sem = s[f"skv{kb}"]
        w0 = (te_free[kb], ccm)
        for r in range(2):
            dview = ksb[kb].rearrange("p (t a o) -> p a t o", a=2, o=128)[:, r, :, :]
            srow = r * rows + kbase + kvh * 128
            rec.dma("sync", dview, kvg[srow:srow + 128, :],
                    waits=w0 if r == 0 else (), sem=sem)
        lv = None
        for j in range(32):
            r, loc = j % 2, j // 2
            srow = r * rows + vbase + kvh * 128
            lv = rec.dma("sync", vsb[kb][:, j * 128:(j + 1) * 128],
                         kvg[srow:srow + 128, loc * 128:(loc + 1) * 128], tr=True,
                         sem=sem)
        return lv

    kld[0] = vld[0] = load_kv(0, 0)
    if nkv > 1:
        kld[1] = vld[1] = load_kv(1, 1)

    seq = [(kvh * qpk + sub, grp)
           for kvh in range(nkv) for sub in range(qpk) for grp in range(NGRP)]

    qmark = [None, None]
    qused = [None, None]

    def load_q(i):
        h, grp = seq[i]
        qb = i % 2
        qmark[qb] = rec.dma("scalar", qt[qb],
                            qd[h * 128:(h + 1) * 128, grp * GW:(grp + 1) * GW],
                            waits=(qused[qb],), sem=s[f"sqt{qb}"])

    load_q(0)
    load_q(1)

    exp_done = [None] * 4
    av_done = [None] * 4
    rc_act = None
    prev_rbm = None
    prev_recip = None
    ctx_free = None
    pend = []
    epi = None

    def emit_av(p):
        av = rec.op(
            "tensor",
            lambda e, vt=vsb[p["kb"]][:, p["j"] * 128:(p["j"] + 1) * 128],
            pp=pt[p["pb"]], f=p["first"], l=p["last"]:
            e.matmul(out=ps[4][:, :], lhsT=vt, rhs=pp, start=f, stop=l),
            inc=s["te"],
            waits=(p["dep"], p["vld"], p["ctx_free"] if p["first"] else None))
        return av

    def emit_den(p):
        den = rec.op(
            "tensor",
            lambda e, pp=pt[p["pb"]], f=p["first"], l=p["last"]:
            e.matmul(out=ps[6][0:1, :], lhsT=ones_bf, rhs=pp, start=f, stop=l),
            inc=s["te"],
            waits=(p["rc_act"] if p["first"] else None, p["ones"]))
        av_done[p["pb"]] = den
        return den

    def emit_epi(pe):
        nonlocal prev_rbm, prev_recip, ctx_free
        rbm = rec.op("tensor", lambda e: e.matmul(
            out=ps[7][:, :], lhsT=ones1p, rhs=rrow, start=True, stop=True),
            inc=s["te"], waits=(pe["rca"], prev_recip))
        prev_rbm = rbm
        rc_ = rec.op("vector", lambda e: e.reciprocal(rbc, ps[7][:, :]),
                     inc=s["dve"], waits=(rbm,))
        prev_recip = rc_
        ctx_sl = bf[:, BA + pe["h"] * T + pe["grp"] * GW:
                    BA + pe["h"] * T + (pe["grp"] + 1) * GW]
        ctx_free = rec.op("vector",
                          lambda e, o=ctx_sl: e.tensor_mul(o, ps[4][:, :], rbc),
                          inc=s["dve"], waits=(rc_,))

    for i, (h, grp) in enumerate(seq):
        kb = (h // qpk) % 2
        qb = i % 2
        jmax = 8 * grp + 8
        last_mm = None
        for j in range(jmax):
            sb = pb = j % 4
            mm_s = rec.op(
                "tensor",
                lambda e, kt=ksb[kb][:, j * 128:(j + 1) * 128], q=qt[qb], b=sb:
                e.matmul(out=ps[b][:, :], lhsT=kt, rhs=q, start=True, stop=True),
                inc=s["te"],
                waits=(kld[kb], qmark[qb], exp_done[sb]))
            last_mm = mm_s
            if j == 0 and epi is not None:
                emit_epi(epi)
                epi = None
            ex = rec.op(
                "scalar",
                lambda e, b=sb, p=pt[pb], hh=h:
                e.activation(out=p, in_=ps[b][:, :], func=AF.Exp,
                             bias=biasall[:, hh:hh + 1], scale=ISQ),
                inc=s["act"], waits=(mm_s, av_done[pb], bias_rdy))
            bias_rdy = None
            exp_done[sb] = ex
            m = j - 8 * grp
            if m >= 0:
                dep = rec.op(
                    "vector",
                    lambda e, p=pt[pb], mk=masks[:, m * 512:(m + 1) * 512]:
                    e.tensor_mul(p, p, mk), inc=s["dve"],
                    waits=(ex, mld))
            else:
                dep = ex
            if len(pend) == 3:
                p0 = pend.pop(0)
                emit_av(p0)
                emit_den(p0)
            pend.append(dict(j=j, kb=kb, pb=pb, dep=dep, first=(j == 0),
                             last=(j == jmax - 1), vld=vld[kb],
                             ctx_free=ctx_free, rc_act=rc_act,
                             ones=onesb_rdy))
            onesb_rdy = None
        qused[qb] = last_mm
        if i + 2 < len(seq):
            load_q(i + 2)
        for p0 in pend:
            emit_av(p0)
            emit_den(p0)
        pend = []
        rca = rec.op("scalar", lambda e: e.activation(
            out=rrow, in_=ps[6][0:1, :], func=AF.Copy), inc=s["act"],
            waits=((s["te"], s["te"].cnt), prev_rbm))
        rc_act = rca
        epi = dict(h=h, grp=grp, rca=rca)
        if grp == NGRP - 1 and h % qpk == qpk - 1:
            hkv = h // qpk
            if hkv + 2 < nkv:
                te_free[kb] = (s["te"], s["te"].cnt)
                kld[kb] = vld[kb] = load_kv(hkv + 2, kb)
    emit_epi(epi)
    epi = None
    rec.barrier()



def ph_oproj(rec, bufs, io, L):
    rec.paranoid = "oproj" in PARANOID
    s = rec.s
    bf, f32, ps = bufs.bf, bufs.f32, bufs.ps
    x_in = io.tensors[f"x{2 * L}"].ap()
    x_out = io.tensors[f"x{2 * L + 1}"].ap()
    wbuf = [bf[:, BB:BB + 4096], bf[:, BB + 4608:BB + 4608 + 4096]]
    scbc = [bf[:, BB + 9216:BB + 9216 + 4096], bf[:, BB + 13312:BB + 13312 + 4096]]
    xold = [f32[:, FS:FS + 512], f32[:, FS + 512:FS + 1024]]
    stgf = [f32[:, FS + 1024:FS + 1536], f32[:, FS + 1536:FS + 2048]]

    if L == 0:
        w_ap = io.ap("woT", (NIC, 128, D), F32, "ExternalInput")
        sc_h = None
    else:
        w_ap = io.ap("qwo", (NIC, 128, D), I32, "ExternalInput")
        sc_h = io.t("sco", (32, 4096), F32, "ExternalInput")

    wld = [None, None]
    te_done = [None, None]
    dq_done = [None, None]
    st_done = [None, None]
    ev_done = [None, None]
    ps_free = [None] * 6
    xld = [None, None]
    pend_evac = None

    def emit_evac(pe):
        oc, mm, bkl = pe
        for grp in range(NGRP):
            b = grp % 2
            gsl = slice(grp * GW, (grp + 1) * GW)
            xld[b] = rec.dma("scalar", xold[b], x_in[oc * 128:(oc + 1) * 128, gsl],
                             waits=(ev_done[b],), sem=s[f"sx{b}"])
            if L == 0:
                bias_col = f32[:, FP + P_BQ + 3 * 32 + oc:FP + P_BQ + 3 * 32 + oc + 1]
                ev = rec.op("vector",
                            lambda e, g=bkl[grp], o=stgf[b], xo=xold[b], bc=bias_col:
                            e.scalar_tensor_tensor(out=o, in0=ps[g][:, :], scalar=bc,
                                                   in1=xo, op0=ALU.add, op1=ALU.add),
                            inc=s["dve"], waits=(mm[grp], xld[b], st_done[b]))
            else:
                ev = rec.op("vector",
                            lambda e, g=bkl[grp], o=stgf[b], xo=xold[b]:
                            e.tensor_add(o, ps[g][:, :], xo),
                            inc=s["dve"], waits=(mm[grp], xld[b], st_done[b]))
            ev_done[b] = ev
            ps_free[bkl[grp]] = ev
            st_done[b] = rec.dma("sync", x_out[oc * 128:(oc + 1) * 128, gsl],
                                 stgf[b], waits=(ev,), sem=s[f"ss{b}"])

    for oc in range(NIC):
        wb = oc % 2
        wld[wb] = rec.dma("gpsimd", wbuf[wb], w_ap[oc],
                          waits=(te_done[wb],), sem=s[f"sw{wb}"])
        if L == 1:
            h0 = rec.dma("gpsimd", scbc[wb][0:16, :],
                         _AP(sc_h.ap().tensor, oc * 4096,
                             [[0, 16], [1, 4096]]),
                         waits=(dq_done[wb],), sem=s[f"sc{wb}"])
            h1 = rec.dma("scalar", scbc[wb][16:32, :], scbc[wb][0:16, :],
                         waits=(h0,), sem=s[f"sc{wb}"])
            h2 = rec.dma("scalar", scbc[wb][32:64, :], scbc[wb][0:32, :],
                         waits=(h1,), sem=s[f"sc{wb}"])
            scld = rec.dma("scalar", scbc[wb][64:128, :], scbc[wb][0:64, :],
                           waits=(h2,), sem=s[f"sc{wb}"])
            dq_done[wb] = rec.op(
                "vector", lambda e, wv=wbuf[wb], sv=scbc[wb]:
                e.scalar_tensor_tensor(out=wv, in0=wv, scalar=-8.0, in1=sv,
                                       op0=ALU.add, op1=ALU.mult),
                inc=s["dve"], waits=(wld[wb], scld, te_done[wb]))
            wrdy = dq_done[wb]
        else:
            wrdy = wld[wb]
        if pend_evac is not None:
            emit_evac(pend_evac)
        mm = [None] * NGRP
        bk = [(oc * NGRP + grp) % 6 for grp in range(NGRP)]
        for ic in range(NIC):
            for grp in range(NGRP):
                waits = (wrdy, ps_free[bk[grp]]) if ic == 0 else ()
                csl = bf[:, BA + ic * T + grp * GW: BA + ic * T + (grp + 1) * GW]
                mm[grp] = rec.op(
                    "tensor",
                    lambda e, wt=wbuf[wb][:, ic * 128:(ic + 1) * 128], cs=csl,
                    f=(ic == 0), l=(ic == NIC - 1), g=bk[grp]:
                    e.matmul(out=ps[g][:, :], lhsT=wt, rhs=cs, start=f, stop=l),
                    inc=s["te"], waits=waits)
        te_done[wb] = mm[NGRP - 1]
        pend_evac = (oc, mm, bk)
    emit_evac(pend_evac)
    rec.barrier()


def ph_ffn0(rec, bufs, io):
    rec.paranoid = "ffn0" in PARANOID
    s = rec.s
    bf, f32, ps = bufs.bf, bufs.f32, bufs.ps
    x_in = io.tensors["x1"].ap()
    x_out = io.tensors["x2"].ap()
    gw_ap = io.ap("gatewT", (D, 28), F32, "ExternalInput")
    a2_ap = io.ap("a2T", (D, 224), F32, "ExternalInput")
    b2_ap = io.ap("b2", (224, D), F32, "ExternalInput")
    bw_ap = io.ap("basewT", (NIC, 128, D), F32, "ExternalInput")
    ex_ap = io.ap("ex28", (28, 224), F32, "ExternalInput")

    wbuf = [bf[:, BB:BB + 4608], bf[:, BB + 4608:BB + 9216]]
    hg = bf[:, BB + 9216:BB + 13312]
    gexs = [f32[:, FS + 5120:FS + 5632], f32[:, FS + 5632:FS + 6144]]
    logits = f32[:, FS + 2048:FS + 2048 + 28]
    m8 = f32[:, FS + 2080:FS + 2088]
    srow = f32[:, FS + 2088:FS + 2096]
    gts = f32[:, FS + 2104:FS + 2132]
    gT = f32[0:28, FS + 2432:FS + 2432 + 2048]
    exp28 = f32[0:28, FS + 2144:FS + 2144 + 224]
    xold = [f32[:, FS:FS + 512], f32[:, FS + 512:FS + 1024]]
    stgf = [f32[:, FS + 1024:FS + 1536], f32[:, FS + 1536:FS + 2048]]
    iden = f32[:, FP + P_IDEN:FP + P_IDEN + 128]
    gwsb = bf[:, BC + 2048:BC + 2048 + 896]

    exld = rec.dma("sync", exp28, ex_ap)
    gwld = rec.dma("gpsimd", gwsb.rearrange("p (ic e) -> p ic e", e=28),
                   gw_ap.rearrange("(ic p) e -> p ic e", p=128))

    prev_lg = None
    prev_gT = None
    prev_tr = None
    for tt in range(NT):
        mm = None
        for ic in range(NIC):
            xsl = bf[:, BA + ic * T + tt * 128: BA + ic * T + (tt + 1) * 128]
            waits = (gwld, prev_lg) if ic == 0 else ()
            mm = rec.op("tensor",
                        lambda e, wt=gwsb[:, ic * 28:(ic + 1) * 28], xs=xsl,
                        f=(ic == 0), l=(ic == NIC - 1):
                        e.matmul(out=ps[0][0:128, 0:28], lhsT=xs, rhs=wt,
                                 start=f, stop=l),
                        inc=s["te"], waits=waits)
        prev_lg = rec.op("vector", lambda e: e.tensor_copy(out=logits, in_=ps[0][:, 0:28]),
                         inc=s["dve"], waits=(mm,), fence=True)
        rec.op("vector", lambda e: e.max(out=m8, in_=logits), fence=True)
        rec.op("vector", lambda e: e.tensor_sub(srow[:, 0:1], m8[:, 1:2], m8[:, 0:1]),
               fence=True)
        dv = rec.op("vector", lambda e: e.tensor_copy(out=srow[:, 1:2], in_=srow[:, 0:1]),
                    inc=s["dve"], fence=True)
        ea = rec.op("scalar", lambda e: e.activation(
            out=srow[:, 1:2], in_=srow[:, 1:2], func=AF.Exp), inc=s["act"], waits=(dv,))
        rec.op("vector", lambda e: e.tensor_scalar(
            out=srow[:, 2:3], in0=srow[:, 1:2], scalar1=1.0, scalar2=None,
            op0=ALU.add), waits=(ea,), fence=True)
        rec.op("vector", lambda e: e.reciprocal(srow[:, 3:4], srow[:, 2:3]),
               fence=True)
        rec.op("vector", lambda e: e.tensor_mul(srow[:, 4:5], srow[:, 1:2], srow[:, 3:4]),
               fence=True)
        rec.op("vector", lambda e: e.tensor_scalar(
            out=gts, in0=logits, scalar1=m8[:, 0:1], scalar2=srow[:, 3:4],
            op0=ALU.is_equal, op1=ALU.mult), waits=(prev_tr,), fence=True)
        rec.op("vector", lambda e: e.tensor_scalar(
            out=logits, in0=logits, scalar1=m8[:, 1:2], scalar2=srow[:, 4:5],
            op0=ALU.is_equal, op1=ALU.mult), fence=True)
        gd = rec.op("vector", lambda e: e.tensor_add(gts, gts, logits),
                    inc=s["dve"])
        tr = rec.op("tensor", lambda e: e.transpose(ps[1][0:28, 0:128], gts, iden),
                    inc=s["te"], waits=(gd, prev_gT))
        prev_tr = tr
        prev_gT = rec.op("vector", lambda e, c=tt: e.tensor_copy(
            out=gT[:, c * 128:(c + 1) * 128], in_=ps[1][0:28, 0:128]),
            inc=s["dve"], waits=(tr,))
    rec.barrier()
    if DBG_FFN:
        dbg_g = io.ap("dbg_gT0", (28, 2048), F32, "ExternalOutput")
        rec.dma("sync", dbg_g, gT)
        rec.barrier()

    ew = [128, 96]
    wldA = [None, None]
    te_dA = [None, None]
    hg_done = {}
    prev_ge = [None, None]
    gex_free = [None, None]
    for ec in range(2):
        wb = ec % 2
        src = (_wtile_src(a2_ap, 0) if ec == 0 else
               a2_ap.rearrange("(ic p) o -> p ic o", p=128)[:, :, 128:224])
        wldA[wb] = rec.dma("gpsimd", wbuf[wb][:, 0:NIC * ew[ec]], src,
                           waits=(te_dA[wb],), sem=s[f"sw{wb}"])
        mm = [None] * NGRP
        for ic in range(NIC):
            for grp in range(NGRP):
                waits = ((wldA[wb], hg_done.get((ec - 1, grp)))
                         if ic == 0 else ())
                xsl = bf[:, BA + ic * T + grp * GW: BA + ic * T + (grp + 1) * GW]
                mm[grp] = rec.op(
                    "tensor",
                    lambda e, wt=wbuf[wb][:, ic * ew[ec]:(ic + 1) * ew[ec]], xs=xsl,
                    f=(ic == 0), l=(ic == NIC - 1), g=grp, w=ew[ec]:
                    e.matmul(out=ps[g][0:w, :], lhsT=wt, rhs=xs, start=f, stop=l),
                    inc=s["te"], waits=waits)
        te_dA[wb] = mm[NGRP - 1]
        for grp in range(NGRP):
            b = grp % 2
            gx = rec.op("tensor",
                        lambda e, g=grp, c=ec, b_=b: e.matmul(
                            out=ps[6 + b_][0:ew[c], :],
                            lhsT=exp28[:, c * 128:c * 128 + ew[c]],
                            rhs=gT[:, g * GW:(g + 1) * GW],
                            start=True, stop=True),
                        inc=s["te"], waits=(mm[grp], exld, prev_ge[b]))
            ge = rec.op("scalar", lambda e, b_=b, c=ec: e.activation(
                out=gexs[b_][0:ew[c], :], in_=ps[6 + b_][0:ew[c], :], func=AF.Copy),
                inc=s["act"], waits=(gx, gex_free[b]))
            prev_ge[b] = ge
            hgm = rec.op("vector",
                         lambda e, g=grp, c=ec, b_=b,
                         o=hg[0:ew[ec], ec * T + grp * GW:ec * T + (grp + 1) * GW]:
                         e.tensor_mul(o, ps[g][0:ew[c], :], gexs[b_][0:ew[c], :]),
                         inc=s["dve"], waits=(ge,))
            hg_done[(ec, grp)] = hgm
            gex_free[b] = hgm

    wld = [None, None]
    te_done = [te_dA[0], te_dA[1]]
    st_done = [None, None]
    ev_done = [None, None]
    ps_free = [None] * 6
    for g in range(NGRP):
        ps_free[g] = hg_done[(1, g)]
    xld = [None, None]
    for oc in range(NIC):
        wb = oc % 2
        rec.dma("gpsimd", wbuf[wb][:, 0:4096], bw_ap[oc],
                waits=(te_done[wb],), sem=s[f"sw{wb}"])
        rec.dma("gpsimd", wbuf[wb][0:128, 4096:4224],
                b2_ap[0:128, oc * 128:(oc + 1) * 128], sem=s[f"sw{wb}"])
        wld[wb] = rec.dma("gpsimd", wbuf[wb][0:96, 4224:4352],
                          b2_ap[128:224, oc * 128:(oc + 1) * 128], sem=s[f"sw{wb}"])
        mm = [None] * NGRP
        bk = [(oc * NGRP + grp) % 6 for grp in range(NGRP)]
        for ic in range(NIC):
            for grp in range(NGRP):
                waits = (wld[wb], ps_free[bk[grp]]) if ic == 0 else ()
                xsl = bf[:, BA + ic * T + grp * GW: BA + ic * T + (grp + 1) * GW]
                mm[grp] = rec.op(
                    "tensor",
                    lambda e, wt=wbuf[wb][:, ic * 128:(ic + 1) * 128], xs=xsl,
                    f=(ic == 0), g=bk[grp]:
                    e.matmul(out=ps[g][:, :], lhsT=wt, rhs=xs, start=f, stop=False),
                    inc=s["te"], waits=waits)
        for ec in range(2):
            for grp in range(NGRP):
                hsl = hg[0:ew[ec], ec * T + grp * GW:ec * T + (grp + 1) * GW]
                mm[grp] = rec.op(
                    "tensor",
                    lambda e,
                    wt=wbuf[wb][0:ew[ec], 4096 + ec * 128:4224 + ec * 128],
                    hs=hsl, l=(ec == 1), g=bk[grp]:
                    e.matmul(out=ps[g][:, :], lhsT=wt, rhs=hs, start=False, stop=l),
                    inc=s["te"],
                    waits=(hg_done[(ec, grp)],) if oc == 0 else ())
        te_done[wb] = mm[NGRP - 1]
        for grp in range(NGRP):
            b = grp % 2
            gsl = slice(grp * GW, (grp + 1) * GW)
            xld[b] = rec.dma("scalar", xold[b], x_in[oc * 128:(oc + 1) * 128, gsl],
                             waits=(ev_done[b],), sem=s[f"sx{b}"])
            ev = rec.op("vector",
                        lambda e, g=bk[grp], o=stgf[b], xo=xold[b]:
                        e.tensor_add(o, ps[g][:, :], xo),
                        inc=s["dve"], waits=(mm[grp], xld[b], st_done[b]))
            ev_done[b] = ev
            ps_free[bk[grp]] = ev
            st_done[b] = rec.dma("sync", x_out[oc * 128:(oc + 1) * 128, gsl],
                                 stgf[b], waits=(ev,), sem=s[f"ss{b}"])
    rec.barrier()


def ph_ffn1(rec, bufs, io):
    rec.paranoid = "ffn1" in PARANOID
    s = rec.s
    bf, f32, ps = bufs.bf, bufs.f32, bufs.ps
    x_in = io.tensors["x3"].ap()
    x_out = io.tensors["x4"].ap()
    a1_ap = io.ap("a1T", (4, 128, D), F32, "ExternalInput")
    b1_ap = io.ap("b1T", (512, D), F32, "ExternalInput")
    wq_ap = io.ap("wqwT", (D, 16), F32, "ExternalInput")
    keys_h = io.t("keysT", (16, 64), F32, "ExternalInput")
    ex_ap = io.ap("ex64", (64, 512), F32, "ExternalInput")

    wbuf = [bf[:, BB:BB + 4608], bf[:, BB + 4608:BB + 9216]]
    hg = bf[:, BB + 9216:BB + 9216 + 8192]
    gexs = [f32[:, FS + 5120:FS + 5632], f32[:, FS + 5632:FS + 6144]]
    sc64 = f32[:, FS + 2048:FS + 2112]
    e64 = f32[:, FS + 2112:FS + 2176]
    m8a = f32[:, FS + 2176:FS + 2184]
    m8b = f32[:, FS + 2184:FS + 2192]
    srow = f32[:, FS + 2192:FS + 2200]
    gT = f32[0:64, FS + 2432:FS + 2432 + 2048]
    exp64 = f32[0:64, FS + 6144:FS + 6144 + 512]
    qfm = bf[0:16, BC + 2048:BC + 2048 + 2048]
    keysb = bf[0:16, BC + 4096:BC + 4096 + 64]
    wqsb = bf[:, BC + 4160:BC + 4160 + 512]
    xold = [f32[:, FS:FS + 512], f32[:, FS + 512:FS + 1024]]
    stgf = [f32[:, FS + 1024:FS + 1536], f32[:, FS + 1536:FS + 2048]]
    iden = f32[:, FP + P_IDEN:FP + P_IDEN + 128]

    exld = rec.dma("sync", exp64, ex_ap)
    kld = rec.dma("gpsimd", keysb, keys_h.ap())
    wqld = rec.dma("gpsimd", wqsb.rearrange("p (ic r) -> p ic r", r=16),
                   wq_ap.rearrange("(ic p) r -> p ic r", p=128))

    prev_qe = None
    for grp in range(NGRP):
        mm = None
        for ic in range(NIC):
            xsl = bf[:, BA + ic * T + grp * GW: BA + ic * T + (grp + 1) * GW]
            waits = (wqld, prev_qe) if ic == 0 else ()
            mm = rec.op("tensor",
                        lambda e, wt=wqsb[:, ic * 16:(ic + 1) * 16], xs=xsl,
                        f=(ic == 0), l=(ic == NIC - 1):
                        e.matmul(out=ps[0][0:16, :], lhsT=wt, rhs=xs, start=f, stop=l),
                        inc=s["te"], waits=waits)
        prev_qe = rec.op("scalar", lambda e, g=grp: e.activation(
            out=qfm[:, g * GW:(g + 1) * GW], in_=ps[0][0:16, :], func=AF.Copy),
            inc=s["act"], waits=(mm,))
    rec.barrier()
    prev_lg = None
    prev_gT = None
    prev_tr = None
    for tt in range(NT):
        mm = rec.op("tensor", lambda e, c=tt: e.matmul(
            out=ps[0][0:128, 0:64], lhsT=qfm[:, c * 128:(c + 1) * 128], rhs=keysb,
            start=True, stop=True), inc=s["te"],
            waits=((wqld,) if tt == 0 else ()) + ((prev_lg,) if prev_lg else ()))
        prev_lg = rec.op("vector", lambda e: e.tensor_copy(out=sc64, in_=ps[0][:, 0:64]),
                         inc=s["dve"], waits=(mm, prev_tr), fence=True)
        rec.op("vector", lambda e: e.max(out=m8a, in_=sc64), fence=True)
        rec.op("vector", lambda e: e.match_replace(
            out=e64, in_to_replace=m8a, in_values=sc64, imm_value=-1e30),
            fence=True)
        rec.op("vector", lambda e: e.max(out=m8b, in_=e64), fence=True)
        rec.op("vector", lambda e: e.tensor_scalar(
            out=srow[:, 0:1], in0=m8a[:, 0:1], scalar1=-1.0, scalar2=None,
            op0=ALU.mult), fence=True)
        dv = rec.op("vector", lambda e: e.tensor_scalar(
            out=e64, in0=sc64, scalar1=m8b[:, 7:8], scalar2=None, op0=ALU.is_ge),
            inc=s["dve"], fence=True)
        ea = rec.op("scalar", lambda e: e.activation(
            out=sc64, in_=sc64, func=AF.Exp, bias=srow[:, 0:1]), inc=s["act"],
            waits=(dv,))
        rec.op("vector", lambda e: e.tensor_mul(sc64, sc64, e64), waits=(ea,),
               fence=True)
        rec.op("vector", lambda e: e.reduce_sum(srow[:, 1:2], sc64, axis=AX.X),
               fence=True)
        rec.op("vector", lambda e: e.reciprocal(srow[:, 2:3], srow[:, 1:2]),
               fence=True)
        gd = rec.op("vector", lambda e: e.tensor_scalar(
            out=sc64, in0=sc64, scalar1=srow[:, 2:3], scalar2=None, op0=ALU.mult),
            inc=s["dve"])
        tr = rec.op("tensor", lambda e: e.transpose(ps[1][0:64, 0:128], sc64, iden),
                    inc=s["te"], waits=(gd, prev_gT))
        prev_tr = tr
        prev_gT = rec.op("vector", lambda e, c=tt: e.tensor_copy(
            out=gT[:, c * 128:(c + 1) * 128], in_=ps[1][0:64, 0:128]),
            inc=s["dve"], waits=(tr,))
    rec.barrier()

    wldA = [None, None]
    te_dA = [None, None]
    hg_done = {}
    prev_ge = [None, None]
    gex_free = [None, None]
    for ec in range(4):
        wb = ec % 2
        wldA[wb] = rec.dma("gpsimd", wbuf[wb][:, 0:4096], a1_ap[ec],
                           waits=(te_dA[wb],), sem=s[f"sw{wb}"])
        mm = [None] * NGRP
        for ic in range(NIC):
            for grp in range(NGRP):
                waits = ((wldA[wb], hg_done.get((ec - 1, grp)))
                         if ic == 0 else ())
                xsl = bf[:, BA + ic * T + grp * GW: BA + ic * T + (grp + 1) * GW]
                mm[grp] = rec.op(
                    "tensor",
                    lambda e, wt=wbuf[wb][:, ic * 128:(ic + 1) * 128], xs=xsl,
                    f=(ic == 0), l=(ic == NIC - 1), g=grp:
                    e.matmul(out=ps[g][:, :], lhsT=wt, rhs=xs, start=f, stop=l),
                    inc=s["te"], waits=waits)
        te_dA[wb] = mm[NGRP - 1]
        for grp in range(NGRP):
            b = grp % 2
            gx = rec.op("tensor",
                        lambda e, g=grp, c=ec, b_=b: e.matmul(
                            out=ps[6 + b_][:, :],
                            lhsT=exp64[:, c * 128:(c + 1) * 128],
                            rhs=gT[:, g * GW:(g + 1) * GW],
                            start=True, stop=True),
                        inc=s["te"], waits=(mm[grp], exld, prev_ge[b]))
            ge = rec.op("scalar", lambda e, b_=b: e.activation(
                out=gexs[b_], in_=ps[6 + b_][:, :], func=AF.Copy),
                inc=s["act"], waits=(gx, gex_free[b]))
            prev_ge[b] = ge
            hgm = rec.op("vector",
                         lambda e, g=grp, c=ec, b_=b:
                         e.tensor_mul(hg[:, c * T + g * GW:c * T + (g + 1) * GW],
                                      ps[g][:, :], gexs[b_]),
                         inc=s["dve"], waits=(ge,))
            hg_done[(ec, grp)] = hgm
            gex_free[b] = hgm

    te_done = [te_dA[0], te_dA[1]]
    st_done = [None, None]
    ev_done = [None, None]
    ps_free = [None] * 6
    for g in range(NGRP):
        ps_free[g] = hg_done[(3, g)]
    xld = [None, None]
    wld = [None, None]
    for oc in range(NIC):
        wb = oc % 2
        wld[wb] = rec.dma(
            "gpsimd",
            wbuf[wb][:, 0:512].rearrange("p (ec o) -> p ec o", o=128),
            b1_ap.rearrange("(ec p) o -> p ec o", p=128)[:, :, oc * 128:(oc + 1) * 128],
            waits=(te_done[wb],), sem=s[f"sw{wb}"])
        mm = [None] * NGRP
        bk = [(oc * NGRP + grp) % 6 for grp in range(NGRP)]
        for ec in range(4):
            for grp in range(NGRP):
                waits = (wld[wb], ps_free[bk[grp]]) if ec == 0 else ()
                if oc == 0:
                    waits = waits + (hg_done[(ec, grp)],)
                hsl = hg[:, ec * T + grp * GW:ec * T + (grp + 1) * GW]
                mm[grp] = rec.op(
                    "tensor",
                    lambda e, wt=wbuf[wb][:, ec * 128:(ec + 1) * 128], hs=hsl,
                    f=(ec == 0), l=(ec == 3), g=bk[grp]:
                    e.matmul(out=ps[g][:, :], lhsT=wt, rhs=hs, start=f, stop=l),
                    inc=s["te"], waits=waits)
        te_done[wb] = mm[NGRP - 1]
        for grp in range(NGRP):
            b = grp % 2
            gsl = slice(grp * GW, (grp + 1) * GW)
            xld[b] = rec.dma("scalar", xold[b], x_in[oc * 128:(oc + 1) * 128, gsl],
                             waits=(ev_done[b],), sem=s[f"sx{b}"])
            ev = rec.op("vector",
                        lambda e, g=bk[grp], o=stgf[b], xo=xold[b]:
                        e.tensor_add(o, ps[g][:, :], xo),
                        inc=s["dve"], waits=(mm[grp], xld[b], st_done[b]))
            ev_done[b] = ev
            ps_free[bk[grp]] = ev
            st_done[b] = rec.dma("sync", x_out[oc * 128:(oc + 1) * 128, gsl],
                                 stgf[b], waits=(ev,), sem=s[f"ss{b}"])
    rec.barrier()



PHASES = {
    "qkv0": lambda rec, bufs, io, fu: (ph_ln(rec, bufs, io, "x0"),
                                       ph_qkv0(rec, bufs, io, fu)),
    "attn0": lambda rec, bufs, io, fu: (ph_attn(rec, bufs, io, 0, fu),
                                        ph_oproj(rec, bufs, io, 0)),
    "ffn0": lambda rec, bufs, io, fu: (ph_ln(rec, bufs, io, "x1"),
                                       ph_ffn0(rec, bufs, io)),
    "qkv1": lambda rec, bufs, io, fu: (ph_ln(rec, bufs, io, "x2"),
                                       ph_qkv1(rec, bufs, io, fu)),
    "attn1": lambda rec, bufs, io, fu: (ph_attn(rec, bufs, io, 1, fu),
                                        ph_oproj(rec, bufs, io, 1)),
    "ffn1": lambda rec, bufs, io, fu: (ph_ln(rec, bufs, io, "x3"),
                                       ph_ffn1(rec, bufs, io)),
}

_DRAM_SPECS = {
    "x0": ((D, T), F32), "x1": ((D, T), F32), "x2": ((D, T), F32),
    "x3": ((D, T), F32), "x4": ((D, T), F32),
    "qd0": ((D, T), BF16), "qd1": ((D, T), BF16),
    "kvl0": ((KV0_ROWS, T), BF16), "kvl1": ((KV1_ROWS, T), BF16),
    "kvg0": ((2 * KV0_ROWS, T), BF16), "kvg1": ((2 * KV1_ROWS, T), BF16),
    "qn0": ((1, 32), F32), "qn1": ((1, 32), F32),
}

_PH_READS = {
    "qkv0": ["x0"], "attn0": ["qd0", "kvg0", "qn0", "x0"], "ffn0": ["x1"],
    "qkv1": ["x2"], "attn1": ["qd1", "kvg1", "qn1", "x2"], "ffn1": ["x3"],
}
_PH_WRITES = {
    "qkv0": ["qd0", "kvl0", "qn0"], "attn0": ["x1"], "ffn0": ["x2"],
    "qkv1": ["qd1", "kvl1", "qn1"], "attn1": ["x3"], "ffn1": ["x4"],
}


def build(phase_names, fused):
    nc = bass.Bass("TRN2", target_bir_lowering=False)
    io = IO(nc)
    reads = set()
    writes = set()
    for p in phase_names:
        reads |= set(_PH_READS[p])
        writes |= set(_PH_WRITES[p])
    if fused:
        writes |= {"kvg0", "kvg1"}
    for name in sorted(reads | writes):
        shape, dt = _DRAM_SPECS[name]
        if name == "x4" and name in writes:
            kind = "ExternalOutput"
        elif name in writes and name in reads:
            kind = "Internal"
        elif name in writes:
            kind = "Internal" if fused else "ExternalOutput"
        else:
            kind = "ExternalInput"
        io.t(name, shape, dt, kind)

    with ExitStack() as st:
        semnames = ["qs", "qg", "qa", "te", "dve", "act", "cc0", "cc1",
                    "sw0", "sw1", "sx0", "sx1", "ss0", "ss1", "ss2", "ss3",
                    "skv0", "skv1", "sqt0", "sqt1", "sv0", "sv1", "sc0", "sc1"]
        sems = {nm: _Sem(st.enter_context(nc.semaphore(nm))) for nm in semnames}
        bfh = st.enter_context(nc.sbuf_tensor("bfa", [128, BF_COLS], BF16))
        f32h = st.enter_context(nc.sbuf_tensor("f32a", [128, F32_COLS], F32))
        psh = [st.enter_context(nc.psum_tensor(f"ps{i}", [128, 512], F32))
               for i in range(8)]
        rec = Rec(nc, sems)
        bufs = Bufs(bfh[:], f32h[:], [p[:] for p in psh])
        rec.btile = bufs.f32[0:1, FP + 460:FP + 461]
        rec.bscrap = bufs.f32[0:1, FP + 461:FP + 462]
        rec.psw = bufs.ps[7][0:1, 0:1]

        ph_prologue(rec, bufs, io)
        for p in phase_names:
            PHASES[p](rec, bufs, io, fused)
        rec.barrier()

        with nc.Block() as block:
            @block.sync
            def _(e):
                for f in rec.streams["sync"]:
                    f(e)

            @block.gpsimd
            def _(e):
                for f in rec.streams["gpsimd"]:
                    f(e)

            @block.tensor
            def _(e):
                for f in rec.streams["tensor"]:
                    f(e)

            @block.vector
            def _(e):
                for f in rec.streams["vector"]:
                    f(e)

            @block.scalar
            def _(e):
                for f in rec.streams["scalar"]:
                    f(e)

    return nc, io



_UNFUSED_PLAN = [["qkv0"], ["attn0"], ["ffn0"], ["qkv1"], ["attn1"], ["ffn1"]]
_FUSED_PLAN = [["qkv0", "attn0", "ffn0", "qkv1", "attn1", "ffn1"]]

PLAN = _FUSED_PLAN


def _run_plan(plan, sh, percore):
    pool = [dict(pc) for pc in percore]
    for c in range(NCORES):
        pool[c].update(sh)
    for launch in plan:
        fused = len(launch) == len(PHASES)
        nc, io = build(launch, fused)
        names = io.input_names
        in_maps = [{n: np.ascontiguousarray(pool[c][n]) for n in names}
                   for c in range(NCORES)]
        res = run_bass_kernel_spmd(nc, in_maps, core_ids=list(range(NCORES)))
        for c in range(NCORES):
            for n in io.output_names:
                pool[c][n] = res.results[c][n]
        for L, nm in ((0, "kvl0"), (1, "kvl1")):
            if nm in io.output_names:
                for c in range(NCORES):
                    p0 = c - c % 2
                    pool[c][f"kvg{L}"] = np.concatenate(
                        [pool[p0][nm], pool[p0 + 1][nm]], axis=0)
    return pool


def kernel(**inputs):
    sh, percore = _prep(inputs)
    pool = _run_plan(PLAN, sh, percore)
    out = np.empty((4, 4096, 4096), dtype=np.float32)
    for c in range(NCORES):
        s, r = c // 2, c % 2
        tiles = [r + 2 * i for i in range(NT)]
        tok = np.concatenate([np.arange(g * 128, (g + 1) * 128) for g in tiles])
        out[s][tok] = np.ascontiguousarray(pool[c]["x4"]).T
    return out
